# version 01
# baseline (speedup 1.0000x reference)
import numpy as np
import ml_dtypes
import concourse.bass as bass
import concourse.mybir as mybir
from concourse.bass_utils import run_bass_kernel_spmd

F32 = mybir.dt.float32
BF16 = mybir.dt.bfloat16
AF = mybir.ActivationFunctionType
ALU = mybir.AluOpType
AX = mybir.AxisListType
BF = ml_dtypes.bfloat16

HID = 4096
NT = 8
TOK = 1024
TOKH = 1027
NCH = 32
PROJ = 18560
NSTRIP = 145
INTER = 8192
NZ = 8
ROWS = [128] * NT + [3]
TGS = [(0, 384), (384, 384), (768, 259)]
ENGS = ("T", "V", "S", "G", "Y")


def _ap(tensor_or_ap, offset, dims):
    t = tensor_or_ap.tensor if isinstance(tensor_or_ap, bass.AP) else tensor_or_ap
    return bass.AP(t, offset, [list(d) for d in dims])


class Sem:
    def __init__(self, nc, ctx, name):
        self.h = ctx.enter_context(nc.semaphore(name))
        self.n = 0


class Builder:
    def __init__(self, n_phase=99, debug=False):
        self.nc = bass.Bass("TRN2", target_bir_lowering=False)
        from contextlib import ExitStack
        self.ctx = ExitStack()
        self.n_phase = n_phase
        self.debug = debug
        self._scopes = {}
        self._tasks = None

    def sb(self, name, shape, dtype, scope=None, side=None):
        cm = self.nc.sbuf_tensor(name, shape, dtype, side=side)
        if scope is None:
            return self.ctx.enter_context(cm)
        t = cm.__enter__()
        self._scopes.setdefault(scope, []).append(cm)
        return t

    def ps(self, name, shape, dtype, scope=None):
        cm = self.nc.psum_tensor(name, shape, dtype)
        if scope is None:
            return self.ctx.enter_context(cm)
        t = cm.__enter__()
        self._scopes.setdefault(scope, []).append(cm)
        return t

    def close_scope(self, scope):
        for cm in reversed(self._scopes.pop(scope, [])):
            cm.__exit__(None, None, None)

    def sem(self, name):
        if not hasattr(self, "_all_sems"):
            self._all_sems = []
            self._sems_by_name = {}
        if name in self._sems_by_name:
            return self._sems_by_name[name]
        s = Sem(self.nc, self.ctx, name)
        self._all_sems.append(s)
        self._sems_by_name[name] = s
        return s

    def begin(self):
        self._tasks = {e: [] for e in ENGS}
        if not hasattr(self, "_seq"):
            self._seq = {e: self.sem("seq" + e) for e in ("T", "V", "S", "G")}
            self._wm = {e: None for e in ("T", "V", "S", "G")}

    def t(self, eng, fn, waits=(), inc=None, inck=1, dma=False):
        waits = [(s.h, v) for (s, v) in waits]
        chainable = eng in ("V", "S", "G") and not dma
        if chainable:
            wm = self._wm[eng]
            if wm is not None:
                waits = list(waits) + [wm]
            if inc is None:
                inc = self._seq[eng]
            self._wm[eng] = (inc.h, inc.n + inck)
        if inc is not None:
            inc.n += inck
        h = inc.h if inc is not None else None

        def run(e):
            for hh, v in waits:
                e.wait_ge(hh, v)
            ins = fn(e)
            if ins is not None and h is not None:
                ins.then_inc(h, inck)
        self._tasks[eng].append(run)

    def fresh(self, sems):
        dirty = [s for s in sems if s.n > 0]
        if not dirty:
            return
        self._clrn = getattr(self, "_clrn", 0) + 1
        with self.nc.Block("clr%d" % self._clrn) as blk:
            @blk.sync
            def _(e):
                for s in dirty:
                    e.sem_clear(s.h)
        for s in dirty:
            s.n = 0

    def drain(self, sems):
        for s in sems:
            if s.n > 0:
                self.t("Y", lambda y: None, waits=[(s, s.n)], dma=True)
                self.t("G", lambda g: None, waits=[(s, s.n)], dma=True)

    def flush(self, name):
        tasks = self._tasks
        self._tasks = None
        with self.nc.Block(name) as blk:
            if tasks["T"]:
                @blk.tensor
                def _(e):
                    for f in tasks["T"]:
                        f(e)
            if tasks["V"]:
                @blk.vector
                def _(e):
                    for f in tasks["V"]:
                        f(e)
            if tasks["S"]:
                @blk.scalar
                def _(e):
                    for f in tasks["S"]:
                        f(e)
            if tasks["G"]:
                @blk.gpsimd
                def _(e):
                    for f in tasks["G"]:
                        f(e)
            if tasks["Y"]:
                @blk.sync
                def _(e):
                    for f in tasks["Y"]:
                        f(e)

    def build(self):
        nc = self.nc
        dbg = "ExternalOutput" if self.debug else "Internal"
        dram = lambda n, s, d, k="ExternalInput": nc.dram_tensor(n, s, d, kind=k).ap()

        self.x_in = dram("x_in", [TOKH, HID], F32)
        self.ip_T = dram("ip_T", [HID, PROJ], F32)
        self.wb_T = dram("wb_T", [HID, HID], F32)
        self.op_T = dram("op_T", [INTER, HID], F32)
        self.w36 = dram("w36", [36, HID], F32)
        self.bcatT = dram("bcatT", [32, HID], F32)
        self.b_rep = dram("b_rep", [128, HID], F32)
        self.normw_col = dram("normw_col", [128, 64], F32)
        self.convw_cols = dram("convw_cols", [128, 320], F32)
        self.convb_cols = dram("convb_cols", [128, 80], F32)
        self.dtb_col = dram("dtb_col", [128, 1], F32)
        self.alog_col = dram("alog_col", [128, 1], F32)
        self.d_bc_in = dram("d_bc_in", [128, 128], F32)
        self.prior_row = dram("prior_row", [1, 4], F32)
        self.rb_row = dram("rb_row", [1, 4], F32)
        self.halo_mask = dram("halo_mask", [4, 1], F32)
        self.selT_in = dram("selT_in", [8, 8], F32)
        self.rmask_in = dram("rmask_in", [8, 1], F32)
        self.ident_bf = dram("ident_bf", [128, 128], BF16)
        self.ident_f32 = dram("ident_f32", [128, 128], F32)
        self.tri01_rep = dram("tri01_rep", [128, 128], F32)
        self.trineg_bf = dram("trineg_bf", [128, 128], BF16)
        self.out_hbm = dram("out", [TOK, HID], F32, "ExternalOutput")

        self.proj_hbm = dram("proj_hbm", [NSTRIP * 128, TOKH], BF16, dbg)
        self.hs_tm_hbm = dram("hs_tm_hbm", [NZ * 128, INTER], BF16, dbg)
        self.s_hbm = dram("s_hbm", [NZ * 128, INTER], BF16, dbg)
        self.normed_hbm = dram("normed_hbm", [INTER, TOK], BF16, dbg)
        self.stream_hbm = dram("stream_hbm", [TOK, HID], BF16, dbg)
        self.lpr_d = dram("lpr_d", [1, 4], F32, "Internal")
        self.cc_a_in = dram("cc_a_in", [1, 128], F32, "Internal")
        self.cc_a_out = dram("cc_a_out", [8, 128], F32, "Internal")
        self.cc_s_in = dram("cc_s_in", [128, INTER], BF16, "Internal")
        self.cc_s_out = dram("cc_s_out", [8 * 128, INTER], BF16, "Internal")
        if self.debug:
            self.dbg_xfm = dram("dbg_xfm", [128, NCH * TOKH], BF16, "ExternalOutput")
            self.dbg_stream = dram("dbg_stream", [128, NT * HID + HID], BF16, "ExternalOutput")
            self.dbg_b = dram("dbg_b", [32, 2048], BF16, "ExternalOutput")
            self.dbg_a = dram("dbg_a", [128, 2048], F32, "ExternalOutput")

        sb = self.sb
        self.consts = sb("consts", [128, 8], F32)
        self.ones_bf = sb("ones_bf", [128, 128], BF16)
        self.id_bf = sb("id_bf", [128, 128], BF16)
        self.id_f32 = sb("id_f32", [128, 128], F32)
        self.dbc = sb("dbc", [128, 128], F32)
        self.hmask = sb("hmask", [4, 1], F32)
        self.tri_rep = sb("tri_rep", [128, 128], F32)
        self.trineg = sb("trineg", [128, 128], BF16)

        self.phase0()
        if self.n_phase >= 1:
            self.ln_block("a")
        if self.n_phase >= 2:
            self.ffn_block(first=True)
        if self.n_phase >= 3:
            self.ln2_spill()
        if self.n_phase >= 4:
            self.inproj_block()
        if self.n_phase >= 5:
            self.conv_block()
            self.states_block()
        if self.n_phase >= 6:
            self.ssd_block()
        if self.n_phase >= 7:
            self.outproj_block()
        if self.n_phase >= 8:
            self.ln3_block()
            self.ffn_block(first=False)
        self.final_block()
        for scope in reversed(list(self._scopes.keys())):
            self.close_scope(scope)
        self.ctx.close()
        return nc

    def phase0(self):
        self.stream = self.sb("stream", [128, NT * HID], BF16, scope="stream")
        self.halo = self.sb("halo", [4, HID], BF16, scope="stream")
        self.xfm = self.sb("xfm", [128, NCH * TOKH], BF16, scope="xfm", side="right")
        self.begin()
        t = self.t
        t("G", lambda g: g.memset(self.consts[:, 0:1], 1e-5))
        t("G", lambda g: g.memset(self.consts[:, 1:2], 1e-12))
        t("G", lambda g: g.memset(self.consts[:, 2:3], 1.0))
        t("G", lambda g: g.memset(self.consts[:, 7:8], 0.0))
        t("G", lambda g: g.memset(self.ones_bf[:, :], 1.0))
        src = _ap(self.x_in, 3 * HID, [[HID, 128], [128 * HID, NT], [1, HID]])
        s_ldg = self.sem("sldg0")
        s_ldy = self.sem("sldy0")
        t("G", lambda g: g.dma_start(self.stream[:, :], src), inc=s_ldg, inck=16, dma=True)
        t("G", lambda g: g.dma_start(self.halo[0:3, :], self.x_in[0:3, :]),
          inc=s_ldg, inck=16, dma=True)
        t("Y", lambda s: s.dma_start(self.id_bf[:, :], self.ident_bf),
          inc=s_ldy, inck=16)
        t("Y", lambda s: s.dma_start(self.id_f32[:, :], self.ident_f32),
          inc=s_ldy, inck=16)
        t("Y", lambda s: s.dma_start(self.dbc[:, :], self.d_bc_in),
          inc=s_ldy, inck=16)
        t("Y", lambda s: s.dma_start(self.hmask[:, :], self.halo_mask),
          inc=s_ldy, inck=16)
        t("Y", lambda s: s.dma_start(self.tri_rep[:, :], self.tri01_rep),
          inc=s_ldy, inck=16)
        t("Y", lambda s: s.dma_start(self.trineg[:, :], self.trineg_bf),
          inc=s_ldy, inck=16)
        t("Y", lambda s: None, waits=[(s_ldy, s_ldy.n), (s_ldg, s_ldg.n)])
        t("G", lambda g: None, waits=[(s_ldy, s_ldy.n), (s_ldg, s_ldg.n)], dma=True)
        self.flush("p0")

    def tile_tm(self, tt):
        return self.stream[:, tt * HID:(tt + 1) * HID] if tt < NT else self.halo[0:3, :]

    def xslice(self, d, tt, r):
        base = TOKH * d + (3 + 128 * tt if tt < NT else 0)
        return self.xfm[:, base:base + r]

    def ln_block(self, tag, src_tiles=None, ntile=9):
        sc = "ln" + tag
        stats = self.sb("stats" + tag, [128, 48], F32, scope=sc)
        mv = self.sb("mv" + tag, [128, 2 * 9], F32, scope=sc)
        rstd = self.sb("rstd" + tag, [128, 9], F32, scope=sc)
        lnout = self.sb("lnout" + tag, [128, 2 * HID], BF16, scope=sc)
        pb = [self.ps(f"pb{tag}{i}", [128, 1024], BF16, scope=sc) for i in range(2)]
        s_ag = self.sem("sag")
        s_rt = self.sem("srt")
        s_no = self.sem("sno")
        s_tp = self.sem("stp")
        s_ev = self.sem("sev")
        self.fresh([s_ag, s_rt, s_no, s_tp, s_ev])
        get = src_tiles if src_tiles is not None else self.tile_tm

        self.begin()
        t = self.t
        for tt in range(ntile):
            r = ROWS[tt]
            srct = get(tt)
            for j in range(8):
                t("V", (lambda v, r=r, j=j, srct=srct:
                        v.bn_stats(stats[0:r, 6 * j:6 * j + 6],
                                   srct[0:r, 512 * j:512 * j + 512])))
            t("V", (lambda v, r=r, tt=tt:
                    v.bn_aggr(mv[0:r, 2 * tt:2 * tt + 2], stats[0:r, 0:48])),
              inc=s_ag)
            t("S", (lambda s, r=r, tt=tt:
                    s.activation(rstd[0:r, tt:tt + 1], mv[0:r, 2 * tt + 1:2 * tt + 2],
                                 AF.Sqrt, bias=self.consts[0:r, 0:1], scale=1.0)),
              waits=[(s_ag, tt + 1)], inc=s_rt)
            t("V", (lambda v, r=r, tt=tt:
                    v.reciprocal(rstd[0:r, tt:tt + 1], rstd[0:r, tt:tt + 1])),
              waits=[(s_rt, tt + 1)])
            buf = (tt % 2) * HID
            t("V", (lambda v, r=r, tt=tt, srct=srct, buf=buf:
                    v.tensor_scalar(lnout[0:r, buf:buf + HID], srct[0:r, :],
                                    mv[0:r, 2 * tt:2 * tt + 1], rstd[0:r, tt:tt + 1],
                                    ALU.subtract, ALU.mult)),
              waits=([(s_tp, 4 * tt)] if tt >= 1 else []), inc=s_no)
            for grp in range(4):
                gidx = 4 * tt + grp
                w = [(s_no, tt + 1)]
                if gidx >= 2:
                    w.append((s_ev, gidx - 1))
                for j in range(8):
                    k = 8 * grp + j
                    t("T", (lambda te, r=r, grp=grp, j=j, k=k, buf=buf:
                            te.transpose(pb[grp % 2][:, 128 * j:128 * j + r],
                                         lnout[0:r, buf + 128 * k:buf + 128 * k + 128],
                                         self.id_bf[0:r, 0:r])),
                      waits=(w if j == 0 else []), inc=(s_tp if j == 7 else None))
                off = TOKH * 8 * grp + (3 + 128 * tt if tt < NT else 0)
                t("S", (lambda s, r=r, grp=grp, off=off:
                        s.copy(
                            _ap(self.xfm[:, :], off, [[NCH * TOKH, 128], [TOKH, 8], [1, r]]),
                            _ap(pb[grp % 2][:, :], 0, [[1024, 128], [128, 8], [1, r]]))),
                  waits=[(s_tp, gidx + 1)], inc=s_ev)
        self.flush("ln" + tag)
        self.close_scope(sc)

    def ffn_block(self, first):
        tag = "f1" if first else "f2"
        sc = "ffn" + tag
        import os as _os2
        ntile = (8 if _os2.environ.get("NTILE8") else 9) if first else 8
        sbl = lambda n, s, d: self.sb(n + tag, s, d, scope=sc)
        psl = lambda n, s, d: self.ps(n + tag, s, d, scope=sc)

        w36nat = sbl("w36nat", [36, HID], BF16)
        w36T = sbl("w36T", [128, NCH * 36], BF16)
        bcat_bf = sbl("bcat_bf", [32, HID], BF16)
        halfb = sbl("halfb", [128, HID], BF16)
        lpr = sbl("lpr", [1, 4], F32)
        rbs = sbl("rbs", [1, 4], F32)
        lpr_rep = sbl("lprrep", [128, 4], F32)
        zlog = sbl("zlog", [128, 9 * 64], F32)
        nmax = sbl("nmax", [128, 9], F32)
        ssum = sbl("ssum", [128, 9], F32)
        alpha = sbl("alpha", [128, 9 * 4], F32)
        zp = sbl("zp", [128, 2 * 32], BF16)
        zpfm = sbl("zpfm", [32, 9 * 128], BF16)
        wstrip = sbl("wstrip", [128, NCH * 512], BF16)
        t1ab = [sbl("t1a", [128, 512], F32), sbl("t1b", [128, 512], F32)]
        if not first:
            oxb = [sbl("oxa", [128, 512], F32), sbl("oxb", [128, 512], F32)]
            s_ox = [self.sem("sox0"), self.sem("sox1")]
            s_oe = self.sem("soe")
        pz = [psl("pza", [128, 512], F32), psl("pzb", [128, 512], F32)]
        pw = [psl(f"pw{i}", [128, 512], F32) for i in range(5)]
        pt = psl("pt", [128, 1024], BF16)

        s_wl = self.sem("swl" + tag)
        s_hl = [self.sem("shl0" + tag), self.sem("shl1" + tag)]
        s_pre = self.sem("spre")
        s_w36 = self.sem("sw36")
        s_zmm = self.sem("szmm")
        s_zc = self.sem("szc")
        s_nm = self.sem("snm")
        s_ex = self.sem("sex")
        s_al = self.sem("sal")
        s_zt = self.sem("szt")
        s_zp = self.sem("szp")
        s_mmw = self.sem("smmw")
        s_evw = self.sem("sevw")
        s_hu = self.sem("shu")
        s_wly = self.sem("swly")
        s_lprd = self.sem("slprd")
        allsems = [s_pre, s_w36, s_zmm, s_zc, s_nm,
                   s_ex, s_al, s_zt, s_zp, s_mmw, s_evw, s_hu, s_wly, s_lprd]
        if not first:
            allsems += [s_ox[0], s_ox[1], s_oe]
        self.fresh(allsems)

        self.begin()
        t = self.t
        t("G", lambda g: g.dma_start(w36nat[0:36, :], self.w36), inc=s_wl, inck=16, dma=True)
        t("G", lambda g: g.dma_start(bcat_bf[:, :], self.bcatT), inc=s_wl, inck=16, dma=True)
        t("Y", lambda s: s.dma_start(lpr[0:1, :], self.prior_row), inc=s_wly, inck=16)
        t("Y", lambda s: s.dma_start(rbs[0:1, :], self.rb_row), inc=s_wly, inck=16)
        t("G", lambda g: g.dma_start(halfb[:, :], self.b_rep), inc=s_wl, inck=16, dma=True)
        LD = s_wl.n
        LDY = s_wly.n
        import os as _os
        for og in ([] if _os.environ.get("SKIP_WB") else range(8)):
            for h in range(2):
                idx = 2 * og + h
                off = (2048 * h) * HID + 512 * og
                dims = [[HID, 128], [128 * HID, 16], [1, 512]]
                w = [(s_hu, idx // 2)] if idx >= 2 else []
                t("G", (lambda g, h=h, off=off, dims=dims:
                        g.dma_start(wstrip[:, 8192 * h:8192 * h + 8192],
                                    _ap(self.wb_T, off, dims))),
                  waits=w, inc=s_hl[h], inck=16, dma=True)
        t("V", lambda v: v.memset(zlog[:, :], 0.0))
        t("V", lambda v: v.memset(alpha[:, :], 0.0))
        t("V", lambda v: v.memset(ssum[:, :], 0.0))
        t("V", lambda v: v.memset(nmax[:, :], 0.0))
        t("V", lambda v: v.memset(zpfm[:, :], 0.0))
        t("V", lambda v: v.memset(zp[:, :], 0.0))
        t("S", lambda s: s.mul(halfb[:, :], halfb[:, :], 0.5),
          waits=[(s_wl, LD), (s_wly, LDY)], inc=s_pre)
        t("S", lambda s: s.activation(lpr[0:1, :], lpr[0:1, :], AF.Ln,
                                      bias=self.consts[0:1, 1:2], scale=1.0),
          inc=s_pre)
        t("V", lambda v: v.tensor_add(lpr[0:1, :], lpr[0:1, :], rbs[0:1, :]),
          waits=[(s_pre, 2)], inc=s_pre)
        t("Y", lambda s: s.dma_start(self.lpr_d, lpr[0:1, :]),
          waits=[(s_pre, 3)], inc=s_lprd, inck=16)
        t("Y", lambda s: s.dma_start(lpr_rep[:, :],
                                     _ap(self.lpr_d, 0, [[0, 128], [1, 4]])),
          waits=[(s_lprd, 16)], inc=s_lprd, inck=16)
        for d in range(NCH):
            t("T", (lambda te, d=d:
                    te.transpose(pt[:, 36 * (d % 8):36 * (d % 8) + 36],
                                 w36nat[0:36, 128 * d:128 * d + 128],
                                 self.id_bf[0:36, 0:36])),
              waits=([(s_wl, LD)] if d == 0 else []) +
                    ([(s_w36, 2 * (d // 8))] if d >= 8 and d % 8 == 0 else []),
              inc=(s_w36 if d % 8 == 7 else None))
            if d % 8 == 7:
                r0 = d // 8
                t("V", (lambda v, r0=r0:
                        v.tensor_copy(w36T[:, 36 * 8 * r0:36 * 8 * r0 + 36 * 8],
                                      pt[:, 0:36 * 8])),
                  waits=[(s_w36, 2 * r0 + 1)], inc=s_w36)
        W36R = s_w36.n
        import os
        SM = int(os.environ.get("SM_LEVEL", "6"))
        rr_ = lambda k: (ROWS[k] if first else 128)
        for tt in range(ntile + 3):
            if tt < ntile:
                r = rr_(tt)
                for d in range(NCH):
                    w = []
                    if d == 0:
                        w = [(s_w36, W36R)] + ([(s_zc, tt - 1)] if tt >= 2 else [])
                    t("T", (lambda te, tt=tt, r=r, d=d:
                            te.matmul(pz[tt % 2][0:r, 0:36], self.xslice(d, tt, r),
                                      w36T[:, 36 * d:36 * d + 36],
                                      start=(d == 0), stop=(d == NCH - 1))),
                      waits=w, inc=(s_zmm if d == NCH - 1 else None))
                t("V", (lambda v, tt=tt, r=r:
                        v.tensor_copy(zlog[0:r, 64 * tt:64 * tt + 36],
                                      pz[tt % 2][0:r, 0:36])),
                  waits=[(s_zmm, tt + 1)], inc=s_zc)
                t("V", (lambda v, tt=tt, r=r:
                        v.tensor_add(zlog[0:r, 64 * tt + 32:64 * tt + 36],
                                     zlog[0:r, 64 * tt + 32:64 * tt + 36],
                                     lpr_rep[0:r, :])),
                  waits=([(s_lprd, 32)] if tt == 0 else []))
                if SM < 2:
                    continue
                t("V", (lambda v, tt=tt, r=r:
                        v.reduce_max(nmax[0:r, tt:tt + 1],
                                     zlog[0:r, 64 * tt + 32:64 * tt + 36],
                                     axis=AX.X, negate=True)), inc=s_nm)
                t("S", (lambda s, tt=tt, r=r:
                        s.activation(alpha[0:r, 4 * tt:4 * tt + 4],
                                     zlog[0:r, 64 * tt + 32:64 * tt + 36], AF.Exp,
                                     bias=nmax[0:r, tt:tt + 1], scale=1.0)),
                  waits=[(s_zc, tt + 1), (s_nm, tt + 1)], inc=s_ex)
            k = tt - 1
            if SM >= 3 and 0 <= k < ntile:
                r = rr_(k)
                t("V", (lambda v, k=k, r=r:
                        v.reduce_sum(ssum[0:r, k:k + 1],
                                     alpha[0:r, 4 * k:4 * k + 4], axis=AX.X)),
                  waits=[(s_ex, k + 1)])
            k = tt - 2
            if SM >= 4 and 0 <= k < ntile:
                r = rr_(k)
                t("V", (lambda v, k=k, r=r:
                        v.reciprocal(ssum[0:r, k:k + 1], ssum[0:r, k:k + 1])))
            k = tt - 3
            if SM >= 5 and 0 <= k < ntile:
                r = rr_(k)
                t("V", (lambda v, k=k, r=r:
                        v.tensor_scalar(alpha[0:r, 4 * k:4 * k + 4],
                                        alpha[0:r, 4 * k:4 * k + 4],
                                        ssum[0:r, k:k + 1], 2.0,
                                        ALU.mult, ALU.mult)))
                t("V", (lambda v, k=k, r=r:
                        v.tensor_tensor(zp[0:r, 32 * (k % 2):32 * (k % 2) + 32],
                                        zlog[0:r, 64 * k:64 * k + 32],
                                        _ap(alpha[:, :], 4 * k, [[36, r], [1, 4], [0, 8]]),
                                        ALU.mult)),
                  waits=([(s_zt, k - 1)] if k >= 2 else []), inc=s_al)
                if SM < 6:
                    continue
                t("T", (lambda te, k=k, r=r:
                        te.transpose(pt[0:32, 128 * (k % 8):128 * (k % 8) + r],
                                     zp[0:r, 32 * (k % 2):32 * (k % 2) + 32],
                                     self.id_bf[0:r, 0:r])),
                  waits=[(s_al, k + 1)] + ([(s_zp, k - 7)] if k >= 8 else []),
                  inc=s_zt)
                t("S", (lambda s, k=k, r=r:
                        s.copy(zpfm[0:32, 128 * k:128 * k + r],
                               pt[0:32, 128 * (k % 8):128 * (k % 8) + r])),
                  waits=[(s_zt, k + 1)], inc=s_zp)
        NZP = s_zp.n
        import os
        if os.environ.get("SKIP_WB"):
            t("V", lambda v: None, waits=[(s_zp, NZP)], dma=True)
            self.ffn_done_sem = s_zp
            self.flush("ffn" + tag)
            self.close_scope(sc)
            return
        waves = [list(range(0, 5)), list(range(5, ntile))] if first \
            else [list(range(0, 4)), list(range(4, 8))]
        for og in range(8):
            for w_i, tl in enumerate(waves):
                widx = 2 * og + w_i
                for d in range(NCH):
                    h = d // 16
                    wts = []
                    if d % 16 == 0:
                        wts.append((s_hl[h], 16 * (og + 1)))
                        if d == 0 and widx >= 1:
                            if first:
                                wts.append((s_evw, widx))
                            else:
                                wts.append((s_oe, 4 * widx))
                    for i, tt in enumerate(tl):
                        r = ROWS[tt] if first else 128
                        last_rd = (w_i == 1 and d == NCH - 1 and i == len(tl) - 1)
                        t("T", (lambda te, i=i, tt=tt, r=r, d=d, h=h:
                                te.matmul(pw[i][0:r, :], self.xslice(d, tt, r),
                                          wstrip[:, 8192 * h + 512 * (d % 16):
                                                 8192 * h + 512 * (d % 16) + 512],
                                          start=(d == 0), stop=False)),
                          waits=(wts if i == 0 else []),
                          inc=(s_hu if last_rd else None))
                for i, tt in enumerate(tl):
                    r = ROWS[tt] if first else 128
                    t("T", (lambda te, i=i, tt=tt, r=r, og=og:
                            te.matmul(pw[i][0:r, :], zpfm[0:32, 128 * tt:128 * tt + r],
                                      bcat_bf[0:32, 512 * og:512 * og + 512],
                                      start=False, stop=True)),
                      waits=([(s_zp, NZP)] if i == 0 else []),
                      inc=(s_mmw if i == len(tl) - 1 else None))
                for i, tt in enumerate(tl):
                    r = ROWS[tt] if first else 128
                    t1 = t1ab[i % 2]
                    t("V", (lambda v, i=i, tt=tt, r=r, og=og, t1=t1:
                            v.scalar_tensor_tensor(t1[0:r, :], pw[i][0:r, :], 0.5,
                                                   halfb[0:r, 512 * og:512 * og + 512],
                                                   ALU.mult, ALU.add)),
                      waits=([(s_mmw, widx + 1)] if i == 0 else []))
                    if first:
                        dstf = (lambda tt, r, og:
                                (self.stream[0:r, HID * tt + 512 * og:HID * tt + 512 * og + 512]
                                 if tt < NT else self.halo[0:3, 512 * og:512 * og + 512]))
                        t("V", (lambda v, tt=tt, r=r, og=og, t1=t1, dstf=dstf:
                                v.tensor_tensor(dstf(tt, r, og), dstf(tt, r, og),
                                                t1[0:r, :], ALU.add)),
                          inc=(s_evw if i == len(tl) - 1 else None))
                    else:
                        cnt = 8 * og + tt
                        ep = cnt % 2
                        wx = [(s_ox[ep], 16 * ((cnt - 2) // 2 + 1))] if cnt >= 2 else []
                        t("V", (lambda v, tt=tt, og=og, t1=t1, ep=ep:
                                v.tensor_tensor(
                                    oxb[ep][:, :],
                                    self.stream2[:, HID * tt + 512 * og:
                                                 HID * tt + 512 * og + 512],
                                    t1[0:128, :], ALU.add)),
                          waits=wx, inc=s_oe)
                        t("Y", (lambda s, tt=tt, og=og, ep=ep:
                                s.dma_start(self.out_hbm[128 * tt:128 * tt + 128,
                                                         512 * og:512 * og + 512],
                                            oxb[ep][:, :])),
                          waits=[(s_oe, s_oe.n)], inc=s_ox[ep], inck=16, dma=True)
            if not first:
                pass
        if not first:
            pass
        if self.debug and first and SM >= 6:
            s_dbg = self.sem("sdbg")
            t("Y", lambda s: s.dma_start(self.dbg_a[:, 0:576], zlog[:, :]),
              waits=[(s_evw, s_evw.n)], inc=s_dbg, inck=16)
            t("Y", lambda s: s.dma_start(self.dbg_a[:, 576:612], alpha[:, :]),
              inc=s_dbg, inck=16)
            t("Y", lambda s: s.dma_start(self.dbg_b[0:32, 0:1152], zpfm[:, :]),
              inc=s_dbg, inck=16)
            t("Y", lambda s: s.dma_start(self.dbg_a[:, 612:616], lpr_rep[:, :]),
              inc=s_dbg, inck=16)
            t("Y", lambda s: s.dma_start(self.dbg_a[:, 616:625], ssum[:, :]),
              inc=s_dbg, inck=16)
            t("Y", lambda s: s.dma_start(self.dbg_a[:, 625:634], nmax[:, :]),
              inc=s_dbg, inck=16)
            t("Y", lambda s: None, waits=[(s_dbg, s_dbg.n)])
        if not first:
            self.drain([s_ox[0], s_ox[1]])
        self.drain([s_wl, s_wly, s_lprd, s_hl[0], s_hl[1]])
        self.ffn_done_sem = s_evw
        self.flush("ffn" + tag)
        self.close_scope(sc)

    def ln2_spill(self):
        self.begin()
        t = self.t
        t("V", lambda v: v.tensor_scalar(self.halo[0:3, :], self.halo[0:3, :],
                                         self.hmask[0:3, 0:1], None, ALU.mult))
        self.flush("mask")
        self.ln_block("b")
        s_sp = self.sem("sspill")
        self.begin()
        for tt in range(NT):
            self.t("Y", (lambda s, tt=tt:
                         s.dma_start(self.stream_hbm[128 * tt:128 * tt + 128, :],
                                     self.stream[:, HID * tt:HID * tt + HID])),
                   inc=s_sp, inck=16)
        self.t("Y", lambda s: None, waits=[(s_sp, s_sp.n)])
        self.t("G", lambda g: None, waits=[(s_sp, s_sp.n)], dma=True)
        self.flush("spill")
        self.close_scope("stream")

    def inproj_block(self):
        sc = "p4"
        self.acum = self.sb("acum", [128, TOK], F32, scope="dt")
        self.negacum = self.sb("negacum", [128, TOK], F32, scope="dt")
        self.lam_bc = self.sb("lam_bc", [128, TOK], F32, scope="dt")
        self.dt_tm = self.sb("dt_tm", [128, TOK], F32, scope="dt")
        self.expa_tm = self.sb("expa_tm", [128, TOK], F32, scope="dt")
        self.dtdec_tm = self.sb("dtdec_tm", [128, TOK], F32, scope="dt2")
        self.aseg = self.sb("aseg", [128, 1], F32, scope="dt2")
        ipstrip = self.sb("ipstrip", [128, 2 * HID], BF16, scope=sc)
        stage = self.sb("ipstage", [128, 2 * TOKH], BF16, scope=sc)
        pso = [self.ps(f"pso{i}", [128, 512], F32, scope=sc) for i in range(6)]
        dt_raw = self.sb("dt_raw", [128, TOKH], F32, scope=sc)
        dt_f = self.sb("dt_f", [128, TOK], F32, scope=sc)
        a_dt = self.sb("a_dt", [128, TOK], F32, scope=sc)
        acum_tm = self.sb("acum_tm", [128, TOK], F32, scope=sc)
        albc = self.sb("albc", [128, TOK], F32, scope=sc)
        tmp4 = self.sb("tmp4", [128, TOK], F32, scope=sc)
        dtb_sb = self.sb("dtb_sb", [128, 1], F32, scope=sc)
        alog_sb = self.sb("alog_sb", [128, 1], F32, scope=sc)
        al8 = self.sb("al8", [128, 8], F32, scope=sc)
        alT = self.sb("alT", [8, 128], F32, scope=sc)
        pdt = [self.ps(f"pdt{i}", [128, 512], F32, scope=sc) for i in range(2)]
        alP0_d = self.nc.dram_tensor("alP0_d", [1, TOK], F32).ap()

        s_ip = [self.sem("sip0"), self.sem("sip1")]
        s_ipu = self.sem("sipu")
        s_evV = self.sem("sevV4")
        s_evS = self.sem("sevS4")
        s_st = [self.sem("sst0"), self.sem("sst1")]
        s_dt = self.sem("sdt")
        s_dty = self.sem("sdty")
        s_dtg = self.sem("sdtg")

        self.begin()
        t = self.t
        t("Y", lambda s: s.dma_start(dtb_sb[:, :], self.dtb_col), inc=s_dty, inck=16, dma=True)
        t("Y", lambda s: s.dma_start(alog_sb[:, :], self.alog_col), inc=s_dty, inck=16, dma=True)
        for s_ in range(NSTRIP):
            par = s_ % 2
            w = [(s_ipu, s_ - 1)] if s_ >= 2 else []
            t("G", (lambda g, s_=s_, par=par:
                    g.dma_start(ipstrip[:, HID * par:HID * par + HID],
                                _ap(self.ip_T, 128 * s_,
                                    [[PROJ, 128], [128 * PROJ, 32], [1, 128]]))),
              waits=w, inc=s_ip[par], inck=16, dma=True)
        for s_ in range(NSTRIP):
            par = s_ % 2
            for gi, (t0, tl) in enumerate(TGS):
                bank = pso[3 * par + gi]
                for d in range(NCH):
                    w = []
                    if d == 0 and gi == 0:
                        w = [(s_ip[par], 16 * (s_ // 2 + 1))]
                        if s_ >= 2:
                            w += [(s_evV, 2 * (s_ - 1)), (s_evS, s_ - 1)]
                    t("T", (lambda te, s_=s_, par=par, gi=gi, t0=t0, tl=tl, d=d, bank=bank:
                            te.matmul(bank[:, 0:tl],
                                      ipstrip[:, HID * par + 128 * d:HID * par + 128 * d + 128],
                                      self.xfm[:, TOKH * d + t0:TOKH * d + t0 + tl],
                                      start=(d == 0), stop=(d == NCH - 1))),
                      waits=w,
                      inc=(s_ipu if (gi == 2 and d == NCH - 1) else None))
            wst = [(s_st[par], 16 * (s_ // 2))] if s_ >= 2 else []
            t("V", (lambda v, s_=s_, par=par:
                    v.tensor_copy(stage[:, TOKH * par:TOKH * par + 384],
                                  pso[3 * par][:, 0:384])),
              waits=[(s_ipu, s_ + 1)] + wst, inc=s_evV)
            t("S", (lambda s, s_=s_, par=par:
                    s.copy(stage[:, TOKH * par + 384:TOKH * par + 768],
                           pso[3 * par + 1][:, 0:384])),
              waits=[(s_ipu, s_ + 1)] + wst, inc=s_evS)
            t("V", (lambda v, s_=s_, par=par:
                    v.tensor_copy(stage[:, TOKH * par + 768:TOKH * par + 1027],
                                  pso[3 * par + 2][:, 0:259])),
              inc=s_evV)
            t("Y", (lambda s, s_=s_, par=par:
                    s.dma_start(self.proj_hbm[128 * s_:128 * s_ + 128, :],
                                stage[:, TOKH * par:TOKH * par + TOKH])),
              waits=[(s_evV, 2 * s_ + 2), (s_evS, s_ + 1)],
              inc=s_st[par], inck=16, dma=True)
            if s_ == NSTRIP - 1:
                t("V", (lambda v, par=par:
                        v.tensor_copy(dt_raw[:, 0:384], pso[3 * par][:, 0:384])),
                  inc=s_evV)
                t("V", (lambda v, par=par:
                        v.tensor_copy(dt_raw[:, 384:768], pso[3 * par + 1][:, 0:384])),
                  inc=s_evV)
                t("V", (lambda v, par=par:
                        v.tensor_copy(dt_raw[:, 768:1027], pso[3 * par + 2][:, 0:259])),
                  inc=s_evV)
        DTE = s_evV.n
        t("V", lambda v: v.tensor_scalar(dt_raw[:, 3:TOKH], dt_raw[:, 3:TOKH],
                                         dtb_sb[:, 0:1], None, ALU.add),
          waits=[(s_evV, DTE), (s_dty, 32)], inc=s_dt)
        vU = s_dt.n
        t("S", lambda s: s.activation(tmp4[:, :], dt_raw[:, 3:TOKH], AF.Abs,
                                      bias=self.consts[:, 7:8], scale=1.0),
          waits=[(s_dt, vU)])
        t("S", lambda s: s.activation(tmp4[:, :], tmp4[:, :], AF.Exp,
                                      bias=self.consts[:, 7:8], scale=-1.0))
        t("S", lambda s: s.activation(tmp4[:, :], tmp4[:, :], AF.Ln,
                                      bias=self.consts[:, 2:3], scale=1.0))
        t("S", lambda s: s.activation(dt_f[:, :], dt_raw[:, 3:TOKH], AF.Relu,
                                      bias=self.consts[:, 7:8], scale=1.0), inc=s_dt)
        sRelu = s_dt.n
        t("V", lambda v: v.tensor_add(dt_f[:, :], dt_f[:, :], tmp4[:, :]),
          waits=[(s_dt, sRelu)])
        t("V", lambda v: v.tensor_scalar(dt_f[:, :], dt_f[:, :], 100.0, None,
                                         ALU.min))
        t("S", lambda s: s.activation(al8[:, 0:1], alog_sb[:, :], AF.Exp,
                                      bias=self.consts[:, 7:8], scale=1.0), inc=s_dt)
        sAE = s_dt.n
        t("V", lambda v: v.tensor_scalar(al8[:, 1:2], al8[:, 0:1], -1.0, None,
                                         ALU.mult), waits=[(s_dt, sAE)])
        t("V", lambda v: v.tensor_scalar(a_dt[:, :], dt_f[:, :], al8[:, 1:2], None,
                                         ALU.mult))
        t("V", lambda v: v.reduce_sum(self.aseg[:, :], a_dt[:, :], axis=AX.X))
        for z in range(NZ):
            t("V", (lambda v, z=z:
                    v.tensor_tensor_scan(self.acum[:, 128 * z:128 * z + 128],
                                         a_dt[:, 128 * z:128 * z + 128],
                                         a_dt[:, 128 * z:128 * z + 128], 0.0,
                                         ALU.add, ALU.bypass)))
        t("V", lambda v: v.tensor_scalar(self.negacum[:, :], self.acum[:, :], -1.0,
                                         None, ALU.mult))
        t("V", lambda v: v.tensor_copy(al8[:, 0:8],
                                       _ap(self.acum[:, :], 127, [[TOK, 128], [128, 8]])),
          inc=s_dt)
        vAL = s_dt.n
        t("T", lambda te: te.transpose(pdt[1][0:8, 0:128], al8[:, 0:8],
                                       self.id_f32[:, :]),
          waits=[(s_dt, vAL)], inc=s_dt)
        tAL = s_dt.n
        t("V", lambda v: v.tensor_copy(alT[0:8, :], pdt[1][0:8, 0:128]),
          waits=[(s_dt, tAL)], inc=s_dt)
        vALT = s_dt.n
        t("G", lambda g: g.dma_start(alP0_d, alT[0:8, :]),
          waits=[(s_dt, vALT)], inc=s_dtg, inck=16, dma=True)
        t("G", lambda g: g.dma_start(albc[:, :], _ap(alP0_d, 0, [[0, 128], [1, TOK]])),
          waits=[(s_dtg, 16)], inc=s_dtg, inck=16, dma=True)
        t("S", lambda s: s.activation(self.lam_bc[:, :], albc[:, :], AF.Exp,
                                      bias=self.consts[:, 7:8], scale=1.0),
          waits=[(s_dtg, 32)])
        evdone = {}
        for j, (srcb, dstb) in enumerate([(self.acum, acum_tm), (dt_f, self.dt_tm)]):
            for z in range(NZ):
                i = j * NZ + z
                wt = [(s_dt, vALT)] if i == 0 else [(s_dt, evdone[i - 1])]
                t("T", (lambda te, srcb=srcb, z=z, i=i:
                        te.transpose(pdt[i % 2][0:128, 0:128],
                                     srcb[:, 128 * z:128 * z + 128],
                                     self.id_f32[:, :])),
                  waits=wt, inc=s_dt)
                tv = s_dt.n
                t("V", (lambda v, dstb=dstb, z=z, i=i:
                        v.tensor_copy(dstb[:, 128 * z:128 * z + 128],
                                      pdt[i % 2][0:128, 0:128])),
                  waits=[(s_dt, tv)], inc=s_dt)
                evdone[i] = s_dt.n
        t("V", lambda v: v.tensor_sub(tmp4[:, :], albc[:, :], acum_tm[:, :]),
          waits=[(s_dtg, 32)], inc=s_dt)
        vS = s_dt.n
        t("S", lambda s: s.activation(tmp4[:, :], tmp4[:, :], AF.Exp,
                                      bias=self.consts[:, 7:8], scale=1.0),
          waits=[(s_dt, vS)], inc=s_dt)
        sE = s_dt.n
        t("V", lambda v: v.tensor_mul(self.dtdec_tm[:, :], self.dt_tm[:, :],
                                      tmp4[:, :]), waits=[(s_dt, sE)])
        t("S", lambda s: s.activation(self.expa_tm[:, :], acum_tm[:, :], AF.Exp,
                                      bias=self.consts[:, 7:8], scale=1.0))
        self.drain([s_st[0], s_st[1], s_ip[0], s_ip[1], s_dtg, s_dty])
        self.flush("p4")
        self.close_scope(sc)

    def conv_block(self):
        self.close_scope("xfm")
        self.bc_post = self.sb("bc_post", [128, 16 * TOK], BF16, scope="bc", side="right")
        sc = "p5a"
        pre = self.sb("cvpre", [128, 2 * TOKH], BF16, scope=sc)
        t1 = self.sb("cvt1", [128, TOK], F32, scope=sc)
        hstile = self.sb("hstile", [128, 2 * TOK], BF16, scope=sc)
        hstg = self.sb("hstg", [128, 2 * TOK], BF16, scope=sc)
        convw_sb = self.sb("convw_sb", [128, 320], F32, scope=sc)
        convb_sb = self.sb("convb_sb", [128, 80], F32, scope=sc)
        pcv = [self.ps(f"pcv{i}", [128, 1024], BF16, scope=sc) for i in range(2)]
        s_cl = [self.sem("scl0"), self.sem("scl1")]
        s_cu = [self.sem("scu0"), self.sem("scu1")]
        s_cw = self.sem("scw")
        s_cs = self.sem("scs")
        s_ct = self.sem("sct")
        s_ce = self.sem("sce")
        s_ch = [self.sem("sch0"), self.sem("sch1")]

        self.begin()
        t = self.t
        t("Y", lambda s: s.dma_start(convw_sb[:, :], self.convw_cols),
          inc=s_cw, inck=16, dma=True)
        t("Y", lambda s: s.dma_start(convb_sb[:, :], self.convb_cols),
          inc=s_cw, inck=16, dma=True)
        for si in range(80):
            par = si % 2
            gs = 64 + si
            w = [(s_cu[par], si // 2)] if si >= 2 else []
            t("Y", (lambda s, si=si, par=par, gs=gs:
                    s.dma_start(pre[:, TOKH * par:TOKH * par + TOKH],
                                self.proj_hbm[128 * gs:128 * gs + 128, :])),
              waits=w, inc=s_cl[par], inck=16, dma=True)
            t("V", (lambda v, si=si, par=par:
                    v.tensor_scalar(t1[:, :], pre[:, TOKH * par:TOKH * par + TOK],
                                    convw_sb[:, 4 * si:4 * si + 1], None, ALU.mult)),
              waits=[(s_cl[par], 16 * (si // 2 + 1)), (s_cw, 32)] +
                    ([(s_cs, si)] if si >= 1 else []))
            for k in range(1, 4):
                t("V", (lambda v, si=si, par=par, k=k:
                        v.scalar_tensor_tensor(
                            t1[:, :], pre[:, TOKH * par + k:TOKH * par + k + TOK],
                            convw_sb[:, 4 * si + k:4 * si + k + 1], t1[:, :],
                            ALU.mult, ALU.add)),
                  inc=(s_cu[par] if k == 3 else None))
            vT1s, vT1 = s_cu[par], s_cu[par].n
            if si < 64:
                hpar = si % 2
                t("S", (lambda s, si=si, hpar=hpar:
                        s.activation(hstile[:, TOK * hpar:TOK * hpar + TOK], t1[:, :],
                                     AF.Silu, bias=convb_sb[:, si:si + 1], scale=1.0)),
                  waits=[(vT1s, vT1)] +
                        ([(s_ct, si - 1)] if si >= 2 else []), inc=s_cs)
                for z in range(NZ):
                    t("T", (lambda te, si=si, hpar=hpar, z=z:
                            te.transpose(pcv[hpar][:, 128 * z:128 * z + 128],
                                         hstile[:, TOK * hpar + 128 * z:TOK * hpar + 128 * z + 128],
                                         self.id_bf[:, :])),
                      waits=([(s_cs, si + 1)] if z == 0 else []) +
                            ([(s_ce, si - 1)] if (z == 0 and si >= 2) else []),
                      inc=(s_ct if z == NZ - 1 else None))
                t("V", (lambda v, si=si, hpar=hpar:
                        v.tensor_copy(hstg[:, TOK * hpar:TOK * hpar + TOK],
                                      pcv[hpar][:, :])),
                  waits=[(s_ct, si + 1)] + ([(s_ch[hpar], 16 * (si // 2))] if si >= 2 else []),
                  inc=s_ce)
                t("Y", (lambda s, si=si, hpar=hpar:
                        s.dma_start(_ap(self.hs_tm_hbm, 128 * si,
                                        [[INTER, 128], [128 * INTER, 8], [1, 128]]),
                                    _ap(hstg[:, :], TOK * hpar,
                                        [[2 * TOK, 128], [128, 8], [1, 128]]))),
                  waits=[(s_ce, si + 1)], inc=s_ch[hpar], inck=16, dma=True)
            else:
                t("S", (lambda s, si=si:
                        s.activation(self.bc_post[:, TOK * (si - 64):TOK * (si - 64) + TOK],
                                     t1[:, :], AF.Silu,
                                     bias=convb_sb[:, si:si + 1], scale=1.0)),
                  waits=[(vT1s, vT1)], inc=s_cs)
        self.drain([s_ch[0], s_ch[1], s_cl[0], s_cl[1], s_cw])
        self.flush("p5a")
        self.close_scope(sc)

    def states_block(self):
        self.prev_bf = self.sb("prev_bf", [128, INTER], BF16, scope="prev", side="right")
        sc = "p5b"
        srun = self.sb("srun", [128, INTER], F32, scope=sc)
        hstm = self.sb("hstm5", [128, 2 * INTER], BF16, scope=sc)
        szst = self.sb("szst", [128, 2 * INTER], BF16, scope=sc)
        btmz = self.sb("btmz", [128, 1024], BF16, scope=sc)
        xdec = self.sb("xdec5", [128, 2 * 512], BF16, scope=sc)
        wsb = self.sb("wsb", [8, 128], F32, scope=sc)
        wbfr = self.sb("wbfr", [8, 128], BF16, scope=sc)
        ag_sb = self.sb("ag_sb", [8, 128], F32, scope=sc)
        selT_sb = self.sb("selT_sb", [8, 8], F32, scope=sc)
        rmask_sb = self.sb("rmask_sb", [8, 1], F32, scope=sc)
        pbt = self.ps("pbt", [128, 1024], BF16, scope=sc)
        pst = [self.ps(f"pst{i}", [128, 512], F32, scope=sc) for i in range(2)]
        pw8 = self.ps("pw8", [128, 512], F32, scope=sc)
        wexp = self.sb("wexp", [8, INTER], BF16, scope=sc)
        wexp_d = self.nc.dram_tensor("wexp_d", [8, INTER], BF16).ap()

        s_hl5 = [self.sem("shl5a"), self.sem("shl5b")]
        s_bt = self.sem("sbt")
        s_sm = self.sem("ssm")
        s_sv = self.sem("ssv")
        s_sz = [self.sem("ssz0"), self.sem("ssz1")]
        s_cc = self.sem("scc")
        s_cg = self.sem("scg")
        s_cy = self.sem("scy")
        s_ck = [self.sem("sck0"), self.sem("sck1")]
        s_wk = [self.sem("swk0"), self.sem("swk1")]

        self.begin()
        t = self.t
        t("V", lambda v: v.memset(srun[:, :], 0.0))
        t("Y", lambda s: s.dma_start(selT_sb[:, :], self.selT_in),
          inc=s_cy, inck=16, dma=True)
        t("Y", lambda s: s.dma_start(rmask_sb[:, :], self.rmask_in),
          inc=s_cy, inck=16, dma=True)
        for z in range(NZ):
            par = z % 2
            t("Y", (lambda s, z=z, par=par:
                    s.dma_start(hstm[:, INTER * par:INTER * par + INTER],
                                self.hs_tm_hbm[128 * z:128 * z + 128, :])),
              waits=([(s_sm, 16 * (z - 1))] if z >= 2 else []),
              inc=s_hl5[par], inck=16, dma=True)
            for g_ in range(8):
                t("T", (lambda te, z=z, g_=g_:
                        te.transpose(pbt[:, 128 * g_:128 * g_ + 128],
                                     self.bc_post[:, 1024 * g_ + 128 * z:1024 * g_ + 128 * z + 128],
                                     self.id_bf[:, :])),
                  waits=([(s_bt, 2 * z)] if z >= 1 else []),
                  inc=(s_bt if g_ == 7 else None))
            t("V", (lambda v, z=z:
                    v.tensor_copy(btmz[:, :], pbt[:, :])),
              waits=[(s_bt, 2 * z + 1)], inc=s_bt)
            for hb in range(16):
                bpar = hb % 2
                for j in range(8):
                    h = 8 * hb + j
                    g_ = h // 16
                    t("V", (lambda v, z=z, par=par, h=h, j=j, bpar=bpar:
                            v.tensor_scalar(xdec[:, 512 * bpar + 64 * j:512 * bpar + 64 * j + 64],
                                            hstm[:, INTER * par + 64 * h:INTER * par + 64 * h + 64],
                                            self.dtdec_tm[:, 128 * z + h:128 * z + h + 1],
                                            None, ALU.mult)),
                      waits=([(s_hl5[par], 16 * (z // 2 + 1)), (s_bt, 2 * z + 2)]
                             if (hb == 0 and j == 0) else []) +
                            ([(s_sv, (16 * z + hb - 2) * 9 + 9)] if (j == 0 and (16 * z + hb) >= 2) else []))
                for j in range(8):
                    h = 8 * hb + j
                    g_ = h // 16
                    t("T", (lambda te, z=z, h=h, j=j, g_=g_, bpar=bpar:
                            te.matmul(pst[bpar][:, 64 * j:64 * j + 64],
                                      btmz[:, 128 * g_:128 * g_ + 128],
                                      xdec[:, 512 * bpar + 64 * j:512 * bpar + 64 * j + 64],
                                      start=True, stop=True)),
                      waits=(([(self._seq["V"], self._seq["V"].n)] +
                              ([(s_sv, ((16 * z + hb) - 2) * 9 + 9)]
                               if (16 * z + hb) >= 2 else [])) if j == 0 else []),
                      inc=(s_sm if j == 7 else None))
                bi = 16 * z + hb
                t("V", (lambda v, z=z, hb=hb, par=par, bpar=bpar:
                        v.tensor_copy(szst[:, INTER * (z % 2) + 512 * hb:INTER * (z % 2) + 512 * hb + 512],
                                      pst[bpar][:, :])),
                  waits=[(s_sm, bi + 1)] +
                        ([(s_sz[z % 2], 16 * (z // 2))] if (hb == 0 and z >= 2) else []),
                  inc=s_sv)
                for j in range(8):
                    h = 8 * hb + j
                    t("V", (lambda v, z=z, h=h, j=j, bpar=bpar:
                            v.scalar_tensor_tensor(srun[:, 64 * h:64 * h + 64],
                                                   srun[:, 64 * h:64 * h + 64],
                                                   self.lam_bc[:, 128 * z + h:128 * z + h + 1],
                                                   pst[bpar][:, 64 * j:64 * j + 64],
                                                   ALU.mult, ALU.add)),
                      inc=s_sv)
            t("Y", (lambda s, z=z:
                    s.dma_start(self.s_hbm[128 * z:128 * z + 128, :],
                                szst[:, INTER * (z % 2):INTER * (z % 2) + INTER])),
              waits=[(s_sv, (16 * z + 15) * 9 + 9)], inc=s_sz[z % 2], inck=16, dma=True)
        t("V", lambda v: v.tensor_copy(hstm[:, 0:INTER], srun[:, :]), inc=s_sv)
        vSEND = s_sv.n
        t("G", lambda g: g.dma_start(self.cc_s_in, hstm[:, 0:INTER]),
          waits=[(s_sv, vSEND)], inc=s_cg, inck=16, dma=True)
        t("G", lambda g: g.dma_start(self.cc_a_in,
                                     _ap(self.aseg[:, :], 0, [[1, 128], [1, 1]])),
          waits=[(s_cg, 16)], inc=s_cg, inck=16, dma=True)
        t("G", lambda g: g.collective_compute(
            "AllGather", ALU.bypass, replica_groups=[list(range(8))],
            ins=[self.cc_a_in], outs=[self.cc_a_out]),
          waits=[(s_cg, 32)], inc=s_cc, dma=True)
        t("G", lambda g: g.collective_compute(
            "AllGather", ALU.bypass, replica_groups=[list(range(8))],
            ins=[self.cc_s_in], outs=[self.cc_s_out]),
          inc=s_cc, dma=True)
        t("Y", lambda s: s.dma_start(ag_sb[:, :], self.cc_a_out),
          waits=[(s_cc, 2), (s_cy, 32)], inc=s_cy, inck=16, dma=True)
        t("T", lambda te: te.matmul(pw8[0:8, 0:128], selT_sb[0:8, 0:8],
                                    ag_sb[0:8, 0:128], start=True, stop=True),
          waits=[(s_cy, 48)], inc=s_sm)
        eMM = s_sm.n
        t("S", lambda s: s.activation(wsb[0:8, :], pw8[0:8, 0:128], AF.Exp,
                                      bias=self.consts[0:8, 7:8], scale=1.0),
          waits=[(s_sm, eMM)], inc=s_sv)
        t("V", lambda v: v.tensor_scalar(wsb[0:8, :], wsb[0:8, :],
                                         rmask_sb[0:8, 0:1], None, ALU.mult),
          waits=[(s_sv, s_sv.n)])
        t("V", lambda v: v.tensor_copy(wbfr[0:8, :], wsb[0:8, :]))
        t("V", lambda v: v.tensor_copy(wexp[0:8, :],
                                       _ap(wbfr, 0, [[128, 8], [1, 128], [0, 64]])),
          inc=s_sv)
        vWB = s_sv.n
        t("G", lambda g: g.dma_start(wexp_d, wexp[0:8, :]),
          waits=[(s_sv, vWB)], inc=s_cg, inck=16, dma=True)
        SZF = [s_sz[0].n, s_sz[1].n]
        for k in range(8):
            par = k % 2
            wfree = ([(s_sv, vWB + k - 1)] if k >= 2 else
                     [(s_cc, 2), (s_cg, 48), (s_sz[0], SZF[0]), (s_sz[1], SZF[1])])
            t("G", (lambda g, k=k, par=par:
                    g.dma_start(szst[:, INTER * par:INTER * par + INTER],
                                _ap(wexp_d, INTER * k, [[0, 128], [1, INTER]]))),
              waits=wfree, inc=s_wk[par], inck=16, dma=True)
            t("G", (lambda g, k=k, par=par:
                    g.dma_start(hstm[:, INTER * par:INTER * par + INTER],
                                self.cc_s_out[128 * k:128 * k + 128, :])),
              inc=s_ck[par], inck=16, dma=True)
            if k == 0:
                t("V", (lambda v, par=par:
                        v.tensor_tensor(srun[:, :], hstm[:, INTER * par:INTER * par + INTER],
                                        szst[:, INTER * par:INTER * par + INTER], ALU.mult)),
                  waits=[(s_wk[par], 16), (s_ck[par], 16)], inc=s_sv)
            else:
                t("V", (lambda v, k=k, par=par:
                        v.tensor_tensor(szst[:, INTER * par:INTER * par + INTER],
                                        hstm[:, INTER * par:INTER * par + INTER],
                                        szst[:, INTER * par:INTER * par + INTER], ALU.mult)),
                  waits=[(s_wk[par], 16 * (k // 2 + 1)), (s_ck[par], 16 * (k // 2 + 1))])
                t("V", (lambda v, par=par:
                        v.tensor_add(srun[:, :], srun[:, :],
                                     szst[:, INTER * par:INTER * par + INTER])),
                  inc=s_sv)
        t("V", lambda v: v.tensor_copy(self.prev_bf[:, :], srun[:, :]))
        self.drain([s_hl5[0], s_hl5[1], s_sz[0], s_sz[1], s_ck[0], s_ck[1],
                    s_wk[0], s_wk[1], s_cg, s_cy])
        self.flush("p5b")
        self.close_scope(sc)

    def ssd_block(self):
        sc = "p6"
        self.close_scope("dt2")
        stk = [self.sb(f"stk{i}", [2, 2048], F32, scope=sc) for i in range(2)]
        hstm = self.sb("hstm6", [128, 2 * INTER], BF16, scope=sc)
        sz6 = self.sb("sz6", [128, INTER], BF16, scope=sc)
        gfm = self.sb("gfm", [128, INTER], BF16, scope=sc)
        stm = self.sb("stm", [128, INTER], BF16, scope=sc)
        ytm = self.sb("ytm", [128, INTER], BF16, scope=sc)
        mtri = self.sb("mtri", [128, 1024], F32, scope=sc)
        lbf = self.sb("lbf", [128, 8 * 128], BF16, scope=sc)
        m2b = self.sb("m2b", [128, 8 * 128], BF16, scope=sc)
        xtm6 = self.sb("xtm6", [128, 8 * 64], BF16, scope=sc)
        t1y = self.sb("t1y", [128, 2 * 512], F32, scope=sc)
        vacc = self.sb("vacc", [128, 8], F32, scope=sc)
        rstd = self.sb("rstd6", [128, 8], F32, scope=sc)
        nw_col = self.sb("nw_col", [128, 64], F32, scope=sc)
        nstg = self.sb("nstg", [128, 2 * 1024], BF16, scope=sc)
        pg = [self.ps(f"pg6{i}", [128, 1024], BF16, scope=sc) for i in range(2)]
        pe6 = [self.ps(f"pe6{i}", [128, 512], F32, scope=sc) for i in range(2)]
        pyd = [self.ps(f"pyd{i}", [128, 512], F32, scope=sc) for i in range(2)]
        pyo = [self.ps(f"pyo{i}", [128, 512], F32, scope=sc) for i in range(2)]

        s_lh = [self.sem("slh0"), self.sem("slh1")]
        s_lz = self.sem("slz")
        s_lg = self.sem("slg")
        s_lk = [self.sem("slk0"), self.sem("slk1")]
        s_nw = self.sem("snw")
        s_gt = self.sem("sgt")
        s_ge = self.sem("sge")
        s_sl = self.sem("ssl")
        s_mr = self.sem("smr")
        s_mt = self.sem("smt")
        s_em = self.sem("sem6")
        s_ex6 = self.sem("sex6")
        s_m2 = self.sem("sm2")
        s_ym = self.sem("sym")
        s_yv = self.sem("syv")
        s_yg = self.sem("syg")
        s_xt = self.sem("sxt")
        s_t1 = self.sem("st1")
        s_gg = self.sem("sgg")
        s_sq = self.sem("ssq")
        s_ne = self.sem("sne")
        s_ns = [self.sem("sns0"), self.sem("sns1")]

        EM = {}; EXh = {}; M2d = {}; XT = {}
        EV = {}; GE = {}; T1 = {}; SQ = {}; GG = {}; NT = {}; NE = {}

        self.begin()
        t = self.t
        t("Y", lambda s: s.dma_start(nw_col[:, :], self.normw_col),
          inc=s_nw, inck=16, dma=True)
        t("V", lambda v: v.memset(stk[0][0:2, :], 1.0))
        t("V", lambda v: v.memset(stk[1][0:2, :], 1.0), inc=s_gg)
        MS0 = s_gg.n

        def stack_load(z, hb):
            gpar = hb % 2
            if hb >= 2:
                w = [(s_em, EM[(z, 8 * (hb - 1) - 1)])]
            elif z >= 1:
                w = [(s_em, EM[(z - 1, 8 * (hb + 15) - 1)])]
            else:
                w = [(s_gg, MS0)]
            t("Y", (lambda s, z=z, hb=hb, gpar=gpar:
                    s.dma_start(stk[gpar][1:2, 0:1024],
                                _ap(self.negacum, 8 * hb * TOK + 128 * z,
                                    [[TOK, 8], [1, 128]]))),
              waits=w, inc=s_lk[gpar], inck=16, dma=True)
            t("Y", (lambda s, z=z, hb=hb, gpar=gpar:
                    s.dma_start(stk[gpar][0:1, 1024:2048],
                                _ap(self.acum, 8 * hb * TOK + 128 * z,
                                    [[TOK, 8], [1, 128]]))),
              inc=s_lk[gpar], inck=16, dma=True)

        for z in range(NZ):
            par = z % 2
            wl = []
            if z >= 2:
                wl = [(s_yg, GE[(z - 2, 15)]), (s_sq, SQ[z - 2])]
            t("Y", (lambda s, z=z, par=par:
                    s.dma_start(hstm[:, INTER * par:INTER * par + INTER],
                                self.hs_tm_hbm[128 * z:128 * z + 128, :])),
              waits=wl, inc=s_lh[par], inck=16, dma=True)
            t("Y", (lambda s, z=z:
                    s.dma_start(sz6[:, :], self.s_hbm[128 * z:128 * z + 128, :])),
              waits=([(s_yv, EV[(z - 1, 15)])] if z >= 1 else []),
              inc=s_lz, inck=16, dma=True)
            t("Y", (lambda s, z=z:
                    s.dma_start(gfm[:, :],
                                _ap(self.proj_hbm, 128 * z + 3,
                                    [[TOKH, 128], [128 * TOKH, 64], [1, 128]]))),
              waits=([(s_gt, NT[z - 1])] if z >= 1 else []),
              inc=s_lg, inck=16, dma=True)
            stack_load(z, 0)
            stack_load(z, 1)
            t("S", (lambda s: s.activation(gfm[:, :], gfm[:, :], AF.Silu,
                                           bias=self.consts[:, 7:8], scale=1.0)),
              waits=[(s_lg, 16 * (z + 1))], inc=s_sl)
            for r_ in range(8):
                for j in range(8):
                    i = 8 * r_ + j
                    w = []
                    if j == 0:
                        w = [(s_sl, z + 1)]
                        if r_ >= 2:
                            w.append((s_ge, s_ge.n - 1))
                        elif z >= 1:
                            w.append((s_ne, NE[z - 1]))
                    t("T", (lambda te, i=i, r_=r_, j=j:
                            te.transpose(pg[r_ % 2][:, 128 * j:128 * j + 128],
                                         gfm[:, 128 * i:128 * i + 128],
                                         self.id_bf[:, :])),
                      waits=w, inc=(s_gt if j == 7 else None))
                t("V", (lambda v, r_=r_:
                        v.tensor_copy(stm[:, 1024 * r_:1024 * r_ + 1024],
                                      pg[r_ % 2][:, :])),
                  waits=[(s_gt, s_gt.n)], inc=s_ge)
            for g_ in range(8):
                w = []
                if g_ == 0:
                    w = [(s_ym, 16 * z)]
                    if z >= 1:
                        w.append((s_ex6, EXh[(z - 1, 127)]))
                if g_ >= 2:
                    w.append((s_mt, 8 * z + g_ - 1))
                t("T", (lambda te, z=z, g_=g_:
                        te.matmul(pe6[g_ % 2][:, 0:128],
                                  self.bc_post[:, 1024 * g_ + 128 * z:1024 * g_ + 128 * z + 128],
                                  self.bc_post[:, 1024 * (8 + g_) + 128 * z:1024 * (8 + g_) + 128 * z + 128],
                                  start=True, stop=True)),
                  waits=w, inc=s_mr)
                t("V", (lambda v, g_=g_:
                        v.scalar_tensor_tensor(mtri[:, 128 * g_:128 * g_ + 128],
                                               pe6[g_ % 2][:, 0:128], 1.0,
                                               self.tri_rep[:, 0:128],
                                               ALU.mult, ALU.mult)),
                  waits=[(s_mr, 8 * z + g_ + 1)], inc=s_mt)
            for hb in range(16):
                bpar = hb % 2
                if hb >= 2:
                    prev_u = (z, hb - 2)
                elif z >= 1:
                    prev_u = (z - 1, hb + 14)
                else:
                    prev_u = None
                for j in range(8):
                    h = 8 * hb + j
                    g_ = h // 16
                    gpar = g_ % 2
                    jj = h - 16 * g_
                    epar = h % 2
                    sl8 = h % 8
                    w = []
                    if j == 0:
                        w.append((s_lk[bpar], 32 * (8 * z + hb // 2 + 1)))
                    if (z, h) > (0, 1):
                        w.append((s_ex6, EXh[(z, h - 2)] if h >= 2 else EXh[(z - 1, 126 + h)]))
                    if h <= 1:
                        w.append((s_mt, 8 * z + 8))
                    t("T", (lambda te, j=j, bpar=bpar, epar=epar:
                            te.matmul(pe6[epar][:, 0:128],
                                      stk[bpar][0:2, 128 * j:128 * j + 128],
                                      stk[bpar][0:2, 1024 + 128 * j:1024 + 128 * j + 128],
                                      start=True, stop=False)),
                      waits=w)
                    t("T", (lambda te, epar=epar:
                            te.matmul(pe6[epar][:, 0:128], self.id_bf[:, :],
                                      self.trineg[:, :], start=False, stop=True)),
                      inc=s_em)
                    EM[(z, h)] = s_em.n
                    w = [(s_em, EM[(z, h)])]
                    if h >= 8:
                        w.append((s_m2, M2d[(z, h - 8)]))
                    elif z >= 1:
                        w.append((s_m2, M2d[(z - 1, 120 + h)]))
                    t("S", (lambda s, epar=epar, sl8=sl8:
                            s.activation(lbf[:, 128 * sl8:128 * sl8 + 128],
                                         pe6[epar][:, 0:128], AF.Exp,
                                         bias=self.consts[:, 7:8], scale=1.0)),
                      waits=w, inc=s_ex6)
                    EXh[(z, h)] = s_ex6.n
                    w = [(s_ex6, EXh[(z, h)]), (s_mt, 8 * z + 8)]
                    if j == 0:
                        w.append((s_ym, 16 * z + hb))
                    t("G", (lambda g, g_=g_, sl8=sl8:
                            g.tensor_tensor(m2b[:, 128 * sl8:128 * sl8 + 128],
                                            lbf[:, 128 * sl8:128 * sl8 + 128],
                                            mtri[:, 128 * g_:128 * g_ + 128],
                                            ALU.mult)),
                      waits=w, inc=s_m2)
                    M2d[(z, h)] = s_m2.n
                    w = []
                    if j == 0:
                        w = [(s_ym, 16 * z + hb)]
                        if h == 0:
                            w.append((s_lh[par], 16 * (z // 2 + 1)))
                    t("V", (lambda v, z=z, h=h, par=par, sl8=sl8:
                            v.tensor_scalar(xtm6[:, 64 * sl8:64 * sl8 + 64],
                                            hstm[:, INTER * par + 64 * h:INTER * par + 64 * h + 64],
                                            self.dt_tm[:, 128 * z + h:128 * z + h + 1],
                                            None, ALU.mult)),
                      waits=w, inc=s_xt)
                    XT[(z, h)] = s_xt.n
                    w = [(s_m2, M2d[(z, h)]), (s_xt, XT[(z, h)])]
                    if j == 0:
                        if prev_u is not None:
                            w.append((s_yg, GE[prev_u]))
                        if z >= 1:
                            w.append((s_yv, EV[(z - 1, hb)]))
                    t("T", (lambda te, j=j, bpar=bpar, sl8=sl8:
                            te.matmul(pyd[bpar][:, 64 * j:64 * j + 64],
                                      m2b[:, 128 * sl8:128 * sl8 + 128],
                                      xtm6[:, 64 * sl8:64 * sl8 + 64],
                                      start=True, stop=True)),
                      waits=w)
                    t("T", (lambda te, h=h, j=j, g_=g_, z=z, bpar=bpar:
                            te.matmul(pyo[bpar][:, 64 * j:64 * j + 64],
                                      self.bc_post[:, 1024 * (8 + g_) + 128 * z:1024 * (8 + g_) + 128 * z + 128],
                                      self.prev_bf[:, 64 * h:64 * h + 64],
                                      start=True, stop=True)),
                      inc=(s_ym if j == 7 else None))
                for j in range(8):
                    h = 8 * hb + j
                    w = []
                    if j == 0:
                        w = [(s_ym, 16 * z + hb + 1)]
                        if prev_u is not None:
                            w.append((s_yg, GE[prev_u]))
                    t("S", (lambda s, z=z, h=h, j=j, bpar=bpar:
                            s.activation(t1y[:, 512 * bpar + 64 * j:512 * bpar + 64 * j + 64],
                                         pyo[bpar][:, 64 * j:64 * j + 64], AF.Copy,
                                         bias=0.0,
                                         scale=self.expa_tm[:, 128 * z + h:128 * z + h + 1])),
                      waits=w, inc=(s_t1 if j == 7 else None))
                T1[(z, hb)] = s_t1.n
                for j in range(8):
                    w = [(s_t1, T1[(z, hb)]), (s_ym, 16 * z + hb + 1)] if j == 0 else []
                    t("V", (lambda v, j=j, bpar=bpar:
                            v.scalar_tensor_tensor(t1y[:, 512 * bpar + 64 * j:512 * bpar + 64 * j + 64],
                                                   pyd[bpar][:, 64 * j:64 * j + 64], 1.0,
                                                   t1y[:, 512 * bpar + 64 * j:512 * bpar + 64 * j + 64],
                                                   ALU.mult, ALU.add)),
                      waits=w)
                for j in range(8):
                    h = 8 * hb + j
                    w = []
                    if j == 0:
                        w = [(s_lz, 16 * (z + 1)), (s_ym, 16 * z + hb + 1)]
                    t("V", (lambda v, z=z, h=h:
                            v.scalar_tensor_tensor(self.prev_bf[:, 64 * h:64 * h + 64],
                                                   self.prev_bf[:, 64 * h:64 * h + 64],
                                                   self.lam_bc[:, 128 * z + h:128 * z + h + 1],
                                                   sz6[:, 64 * h:64 * h + 64],
                                                   ALU.mult, ALU.add)),
                      waits=w, inc=(s_yv if j == 7 else None))
                EV[(z, hb)] = s_yv.n
                for j in range(8):
                    h = 8 * hb + j
                    w = []
                    if j == 0:
                        w = [(s_yv, EV[(z, hb)])]
                        if z >= 1 and hb == 0:
                            w.append((s_gg, GG[z - 1]))
                    t("G", (lambda g, h=h, par=par:
                            g.tensor_scalar(ytm[:, 64 * h:64 * h + 64],
                                            hstm[:, INTER * par + 64 * h:INTER * par + 64 * h + 64],
                                            self.dbc[:, h:h + 1], None, ALU.mult)),
                      waits=w)
                    t("G", (lambda g, h=h, j=j, bpar=bpar:
                            g.tensor_add(ytm[:, 64 * h:64 * h + 64],
                                         ytm[:, 64 * h:64 * h + 64],
                                         t1y[:, 512 * bpar + 64 * j:512 * bpar + 64 * j + 64])),
                      inc=(s_yg if j == 7 else None))
                GE[(z, hb)] = s_yg.n
                if hb + 2 <= 15:
                    stack_load(z, hb + 2)
            t("V", (lambda v: v.tensor_tensor(stm[:, :], ytm[:, :], stm[:, :],
                                              ALU.mult)),
              waits=[(s_yg, GE[(z, 15)]), (s_ge, s_ge.n)], inc=s_gg)
            vG = s_gg.n
            GG[z] = vG
            for gr in range(8):
                t("S", (lambda s, gr=gr, par=par:
                        s.activation(hstm[:, INTER * par + 1024 * gr:INTER * par + 1024 * gr + 1024],
                                     stm[:, 1024 * gr:1024 * gr + 1024], AF.Square,
                                     accum_out=vacc[:, gr:gr + 1])),
                  waits=([(s_gg, vG)] if gr == 0 else []))
            t("S", (lambda s: s.copy(rstd[:, 0:1], vacc[:, 0:1])))
            t("S", (lambda s: s.activation(rstd[:, :], vacc[:, :], AF.Ln,
                                           bias=self.consts[:, 0:1],
                                           scale=1.0 / 1024.0)))
            t("S", (lambda s: s.activation(rstd[:, :], rstd[:, :], AF.Exp,
                                           bias=self.consts[:, 7:8], scale=-0.5)),
              inc=s_sq)
            SQ[z] = s_sq.n
            for gr in range(8):
                t("V", (lambda v, gr=gr:
                        v.tensor_scalar(gfm[:, 1024 * gr:1024 * gr + 1024],
                                        stm[:, 1024 * gr:1024 * gr + 1024],
                                        rstd[:, gr:gr + 1], None, ALU.mult)),
                  waits=([(s_sq, SQ[z])] if gr == 0 else []), inc=s_gg)
            vNM = s_gg.n
            for r_ in range(8):
                for j in range(8):
                    i = 8 * r_ + j
                    w = []
                    if j == 0:
                        w = [(s_gg, vNM)]
                        if r_ >= 2:
                            w.append((s_ne, s_ne.n - 1))
                        else:
                            w.append((s_ge, 8 * (z + 1)))
                    t("T", (lambda te, i=i, r_=r_, j=j:
                            te.transpose(pg[r_ % 2][:, 128 * j:128 * j + 128],
                                         gfm[:, 128 * i:128 * i + 128],
                                         self.id_bf[:, :])),
                      waits=w, inc=(s_gt if j == 7 else None))
                for j in range(8):
                    i = 8 * r_ + j
                    t("V", (lambda v, i=i, j=j, r_=r_:
                            v.tensor_scalar(nstg[:, 1024 * (r_ % 2) + 128 * j:1024 * (r_ % 2) + 128 * j + 128],
                                            pg[r_ % 2][:, 128 * j:128 * j + 128],
                                            nw_col[:, i:i + 1], None, ALU.mult)),
                      waits=([(s_gt, s_gt.n), (s_nw, 16)] +
                             ([(s_ns[r_ % 2], 16 * (4 * z + r_ // 2))] if (8 * z + r_) >= 2 else []))
                      if j == 0 else [],
                      inc=(s_ne if j == 7 else None))
                t("Y", (lambda s, z=z, r_=r_:
                        s.dma_start(_ap(self.normed_hbm, 1024 * r_ * TOK + 128 * z,
                                        [[TOK, 128], [128 * TOK, 8], [1, 128]]),
                                    _ap(nstg[:, :], 1024 * (r_ % 2),
                                        [[2048, 128], [128, 8], [1, 128]]))),
                  waits=[(s_ne, s_ne.n)], inc=s_ns[r_ % 2], inck=16, dma=True)
            NT[z] = s_gt.n
            NE[z] = s_ne.n
        self.drain([s_lh[0], s_lh[1], s_lz, s_lg, s_lk[0], s_lk[1],
                    s_ns[0], s_ns[1], s_nw])
        self.flush("p6")
        self.close_scope(sc)
        self.close_scope("prev")
        self.close_scope("bc")
        self.close_scope("dt")

    def outproj_block(self):
        sc = "p7"
        nfm = self.sb("nfm", [128, 64 * TOK], BF16, scope=sc)
        wq = self.sb("wq7", [128, 2 * 8192], BF16, scope=sc)
        sx = self.sb("sx7", [128, 2 * 512], BF16, scope=sc)
        x2b = self.sb("x2b", [128, 2 * 512], BF16, scope=sc)
        pw7 = [self.ps(f"pw7{i}", [128, 512], F32, scope=sc) for i in range(8)]
        s_nf = self.sem("snf7")
        s_wq = [self.sem("swq0"), self.sem("swq1")]
        s_qd = self.sem("sipu")
        s_od = self.sem("ssm")
        s_ev7 = self.sem("sce")
        s_sx = [self.sem("sst0"), self.sem("sst1")]
        s_x2 = [self.sem("sch0"), self.sem("sch1")]
        self.fresh([s_qd, s_od, s_ev7,
                    s_sx[0], s_sx[1], s_x2[0], s_x2[1]])

        self.begin()
        t = self.t
        t("G", lambda g: g.dma_start(
            nfm[:, :], _ap(self.normed_hbm, 0,
                           [[TOK, 128], [128 * TOK, 64], [1, TOK]])),
          inc=s_nf, inck=16, dma=True)
        for og in range(8):
            for qq in range(4):
                idx = 4 * og + qq
                par = idx % 2
                w = [(s_qd, idx - 1)] if idx >= 2 else []
                t("G", (lambda g, og=og, qq=qq, par=par:
                        g.dma_start(wq[:, 8192 * par:8192 * par + 8192],
                                    _ap(self.op_T, 2048 * qq * HID + 512 * og,
                                        [[HID, 128], [128 * HID, 16], [1, 512]]))),
                  waits=w, inc=s_wq[par], inck=16, dma=True)
            for qq in range(4):
                idx = 4 * og + qq
                par = idx % 2
                for i16 in range(16):
                    ic = 16 * qq + i16
                    w = []
                    if i16 == 0:
                        w = [(s_wq[par], 16 * (idx // 2 + 1))]
                        if qq == 0:
                            w.append((s_nf, 16))
                            if og >= 1:
                                w.append((s_ev7, 8 * og))
                    for tt in range(8):
                        t("T", (lambda te, og=og, ic=ic, tt=tt, par=par, i16=i16:
                                te.matmul(pw7[tt][:, :],
                                          nfm[:, 1024 * ic + 128 * tt:1024 * ic + 128 * tt + 128],
                                          wq[:, 8192 * par + 512 * i16:8192 * par + 512 * i16 + 512],
                                          start=(ic == 0), stop=(ic == 63))),
                          waits=(w if tt == 0 else []),
                          inc=(s_qd if (i16 == 15 and tt == 7) else None))
            OD = s_qd.n
            for tt in range(8):
                ep = tt % 2
                cnt = 8 * og + tt
                t("Y", (lambda s, og=og, tt=tt, ep=ep:
                        s.dma_start(sx[:, 512 * ep:512 * ep + 512],
                                    self.stream_hbm[128 * tt:128 * tt + 128,
                                                    512 * og:512 * og + 512])),
                  waits=([(s_ev7, cnt - 1)] if cnt >= 2 else []),
                  inc=s_sx[ep], inck=16, dma=True)
                t("V", (lambda v, tt=tt, ep=ep:
                        v.tensor_tensor(x2b[:, 512 * ep:512 * ep + 512],
                                        pw7[tt][:, :], sx[:, 512 * ep:512 * ep + 512],
                                        ALU.add)),
                  waits=[(s_qd, OD), (s_sx[ep], 16 * (s_sx[ep].n // 16))] +
                        ([(s_x2[ep], 16 * ((cnt - 2) // 2 + 1))] if cnt >= 2 else []),
                  inc=s_ev7)
                t("Y", (lambda s, og=og, tt=tt, ep=ep:
                        s.dma_start(self.stream_hbm[128 * tt:128 * tt + 128,
                                                    512 * og:512 * og + 512],
                                    x2b[:, 512 * ep:512 * ep + 512])),
                  waits=[(s_ev7, s_ev7.n)], inc=s_x2[ep], inck=16, dma=True)
        self.drain([s_nf, s_wq[0], s_wq[1], s_sx[0], s_sx[1], s_x2[0], s_x2[1]])
        self.flush("p7")
        self.close_scope(sc)

    def ln3_block(self):
        self.stream2 = self.sb("stream2", [128, NT * HID], BF16, scope="p89")
        self.xfm = self.sb("xfm2", [128, NCH * TOKH], BF16, scope="p89x", side="right")
        s_s2 = self.sem("sspill")
        self.fresh([s_s2])
        self.begin()
        for tt in range(NT):
            self.t("Y", (lambda s, tt=tt:
                         s.dma_start(self.stream2[:, HID * tt:HID * tt + HID],
                                     self.stream_hbm[128 * tt:128 * tt + 128, :])),
                   inc=s_s2, inck=16, dma=True)
        self.t("Y", lambda s: None, waits=[(s_s2, s_s2.n)], dma=True)
        self.t("G", lambda g: None, waits=[(s_s2, s_s2.n)], dma=True)
        self.flush("p8ld")
        self.ln_block("c", src_tiles=lambda tt: self.stream2[:, tt * HID:(tt + 1) * HID],
                      ntile=8)

    def final_block(self):
        s_fin = self.sem("sfin")
        self.begin()
        for s in self._all_sems[:-1]:
            if s.n > 0:
                self.t("Y", lambda y: None, waits=[(s, s.n)])
        if self.debug and self.n_phase >= 1 and "xfm" in self._scopes:
            self.t("Y", lambda s: s.dma_start(self.dbg_xfm[:, :], self.xfm[:, :]),
                   inc=s_fin, inck=16)
        if self.debug and "prev" in self._scopes:
            self.t("G", lambda g: g.dma_start(self.dbg_a[:, 0:2048],
                                              self.prev_bf[:, 0:2048]),
                   inc=s_fin, inck=16, dma=True)
        if self.debug and "stream" in self._scopes:
            self.t("Y", lambda s: s.dma_start(self.dbg_stream[:, 0:NT * HID],
                                              self.stream[:, :]), inc=s_fin, inck=16)
            self.t("Y", lambda s: s.dma_start(self.dbg_stream[0:3, NT * HID:],
                                              self.halo[0:3, :]), inc=s_fin, inck=16)
        if s_fin.n:
            self.t("Y", lambda s: None, waits=[(s_fin, s_fin.n)])
        self.flush("fin")


def host_prep(inputs):
    f32 = np.float32
    x = np.asarray(inputs["L0_attn_x"], f32)
    ipw = np.asarray(inputs["L0_attn_in_proj_w"], f32)
    convw = np.asarray(inputs["L0_attn_conv_w"], f32)
    convb = np.asarray(inputs["L0_attn_conv_b"], f32)
    dtb = np.asarray(inputs["L0_attn_dt_bias"], f32)
    alog = np.asarray(inputs["L0_attn_A_log"], f32)
    dd = np.asarray(inputs["L0_attn_D"], f32)
    normw = np.asarray(inputs["L0_attn_norm_w"], f32)
    opw = np.asarray(inputs["L0_attn_out_proj_w"], f32)
    wb = np.asarray(inputs["L0_ffn_W_base"], f32)
    bb = np.asarray(inputs["L0_ffn_b_base"], f32)
    lA = np.asarray(inputs["L0_ffn_lora_A"], f32)
    lB = np.asarray(inputs["L0_ffn_lora_B"], f32)
    rW = np.asarray(inputs["L0_ffn_router_W"], f32)
    rb = np.asarray(inputs["L0_ffn_router_b"], f32)
    prior = np.asarray(inputs["L0_ffn_domain_prior"], f32)
    act = np.asarray(inputs["L0_ffn_active"]).astype(np.int64)

    shared = {
        "ip_T": np.ascontiguousarray(ipw.T),
        "wb_T": np.ascontiguousarray(wb.T),
        "op_T": np.ascontiguousarray(opw.T),
        "w36": np.ascontiguousarray(
            np.concatenate([lA[act].reshape(32, HID), rW[act]], axis=0)),
        "bcatT": np.ascontiguousarray(
            lB[act].transpose(0, 2, 1).reshape(32, HID)),
        "b_rep": np.ascontiguousarray(np.tile(bb[None, :], (128, 1))),
        "normw_col": np.ascontiguousarray(normw.reshape(64, 128).T),
        "convw_cols": np.ascontiguousarray(
            convw.reshape(80, 128, 4).transpose(1, 0, 2).reshape(128, 320)),
        "convb_cols": np.ascontiguousarray(
            convb.reshape(80, 128).T),
        "dtb_col": np.ascontiguousarray(dtb[:, None]),
        "alog_col": np.ascontiguousarray(alog[:, None]),
        "d_bc_in": np.ascontiguousarray(np.tile(dd[None, :], (128, 1))),
        "prior_row": np.ascontiguousarray(prior[act][None, :]),
        "rb_row": np.ascontiguousarray(rb[act][None, :]),
        "ident_bf": np.eye(128, dtype=BF),
        "ident_f32": np.eye(128, dtype=f32),
        "tri01_rep": np.ascontiguousarray(np.triu(np.ones((128, 128), f32))),
        "trineg_bf": np.where(np.triu(np.ones((128, 128), bool)),
                              np.zeros((128, 128)), -1e9).astype(BF),
    }
    per_core = []
    for c in range(8):
        b, q = c // 4, c % 4
        xs = np.zeros((TOKH, HID), f32)
        t0 = 1024 * q
        xs[3:, :] = x[b, t0:t0 + 1024, :]
        if q > 0:
            xs[0:3, :] = x[b, t0 - 3:t0, :]
        hm = np.zeros((4, 1), f32)
        hm[:, 0] = 0.0 if q == 0 else 1.0
        selT = np.zeros((8, 8), f32)
        rmask = np.zeros((8, 1), f32)
        for k in range(8):
            if k // 4 == b and (k % 4) < q:
                rmask[k, 0] = 1.0
                for m in range(8):
                    if m // 4 == b and (k % 4) < (m % 4) < q:
                        selT[m, k] = 1.0
        d = dict(shared)
        d.update({"x_in": xs, "halo_mask": hm, "selT_in": selT, "rmask_in": rmask})
        per_core.append(d)
    return per_core


_CACHED = {}


def _get_nc(n_phase=99, debug=False):
    key = (n_phase, debug)
    if key not in _CACHED:
        _CACHED[key] = Builder(n_phase, debug).build()
    return _CACHED[key]


def kernel(**inputs):
    per_core = host_prep(inputs)
    nc = _get_nc()
    res = run_bass_kernel_spmd(nc, per_core, core_ids=list(range(8)))
    outs = [res.results[c]["out"] for c in range(8)]
    full = np.zeros((2, 4096, HID), np.float32)
    for c in range(8):
        b, q = c // 4, c % 4
        full[b, 1024 * q:1024 * q + 1024, :] = outs[c]
    return full


# version 02 (final)
# speedup vs baseline: 1.4764x; 1.4764x over previous
import numpy as np
import ml_dtypes
import concourse.bass as bass
import concourse.mybir as mybir
from concourse.bass_utils import run_bass_kernel_spmd

F32 = mybir.dt.float32
BF16 = mybir.dt.bfloat16
AF = mybir.ActivationFunctionType
ALU = mybir.AluOpType
AX = mybir.AxisListType
BF = ml_dtypes.bfloat16

HID = 4096
NT = 8
TOK = 1024
TOKH = 1027
NCH = 32
PROJ = 18560
NSTRIP = 145
INTER = 8192
NZ = 8
ROWS = [128] * NT + [3]
TGS = [(0, 384), (384, 384), (768, 259)]
ENGS = ("T", "V", "S", "G", "Y")


def _ap(tensor_or_ap, offset, dims):
    t = tensor_or_ap.tensor if isinstance(tensor_or_ap, bass.AP) else tensor_or_ap
    return bass.AP(t, offset, [list(d) for d in dims])


class Sem:
    def __init__(self, nc, ctx, name):
        self.h = ctx.enter_context(nc.semaphore(name))
        self.n = 0


class Builder:
    def __init__(self, n_phase=99, debug=False):
        self.nc = bass.Bass("TRN2", target_bir_lowering=False)
        from contextlib import ExitStack
        self.ctx = ExitStack()
        self.n_phase = n_phase
        self.debug = debug
        self._scopes = {}
        self._tasks = None

    def sb(self, name, shape, dtype, scope=None, side=None):
        cm = self.nc.sbuf_tensor(name, shape, dtype, side=side)
        if scope is None:
            return self.ctx.enter_context(cm)
        t = cm.__enter__()
        self._scopes.setdefault(scope, []).append(cm)
        return t

    def ps(self, name, shape, dtype, scope=None):
        cm = self.nc.psum_tensor(name, shape, dtype)
        if scope is None:
            return self.ctx.enter_context(cm)
        t = cm.__enter__()
        self._scopes.setdefault(scope, []).append(cm)
        return t

    def close_scope(self, scope):
        for cm in reversed(self._scopes.pop(scope, [])):
            cm.__exit__(None, None, None)

    def sem(self, name):
        if not hasattr(self, "_all_sems"):
            self._all_sems = []
            self._sems_by_name = {}
        if name in self._sems_by_name:
            return self._sems_by_name[name]
        s = Sem(self.nc, self.ctx, name)
        self._all_sems.append(s)
        self._sems_by_name[name] = s
        return s

    def begin(self):
        self._tasks = {e: [] for e in ENGS}
        if not hasattr(self, "_seq"):
            self._seq = {e: self.sem("seq" + e) for e in ("T", "V", "S", "G")}
            self._wm = {e: None for e in ("T", "V", "S", "G")}

    def t(self, eng, fn, waits=(), inc=None, inck=1, dma=False):
        waits = [(s.h, v) for (s, v) in waits]
        chainable = eng in ("V", "S", "G") and not dma
        if chainable:
            wm = self._wm[eng]
            if wm is not None:
                waits = list(waits) + [wm]
            if inc is None:
                inc = self._seq[eng]
            self._wm[eng] = (inc.h, inc.n + inck)
        if inc is not None:
            inc.n += inck
        h = inc.h if inc is not None else None

        def run(e):
            for hh, v in waits:
                e.wait_ge(hh, v)
            ins = fn(e)
            if ins is not None and h is not None:
                ins.then_inc(h, inck)
        self._tasks[eng].append(run)

    def fresh(self, sems):
        dirty = [s for s in sems if s.n > 0]
        if not dirty:
            return
        self._clrn = getattr(self, "_clrn", 0) + 1
        with self.nc.Block("clr%d" % self._clrn) as blk:
            @blk.sync
            def _(e):
                for s in dirty:
                    e.sem_clear(s.h)
        for s in dirty:
            s.n = 0

    def drain(self, sems):
        for s in sems:
            if s.n > 0:
                self.t("Y", lambda y: None, waits=[(s, s.n)], dma=True)
                self.t("G", lambda g: None, waits=[(s, s.n)], dma=True)

    def flush(self, name):
        tasks = self._tasks
        self._tasks = None
        with self.nc.Block(name) as blk:
            if tasks["T"]:
                @blk.tensor
                def _(e):
                    for f in tasks["T"]:
                        f(e)
            if tasks["V"]:
                @blk.vector
                def _(e):
                    for f in tasks["V"]:
                        f(e)
            if tasks["S"]:
                @blk.scalar
                def _(e):
                    for f in tasks["S"]:
                        f(e)
            if tasks["G"]:
                @blk.gpsimd
                def _(e):
                    for f in tasks["G"]:
                        f(e)
            if tasks["Y"]:
                @blk.sync
                def _(e):
                    for f in tasks["Y"]:
                        f(e)

    def build(self):
        nc = self.nc
        dbg = "ExternalOutput" if self.debug else "Internal"
        dram = lambda n, s, d, k="ExternalInput": nc.dram_tensor(n, s, d, kind=k).ap()

        self.x_in = dram("x_in", [TOKH, HID], F32)
        self.ip_T = dram("ip_T", [HID, PROJ], F32)
        self.wb_T = dram("wb_T", [HID, HID], F32)
        self.op_T = dram("op_T", [INTER, HID], F32)
        self.w36 = dram("w36", [36, HID], F32)
        self.bcatT = dram("bcatT", [32, HID], F32)
        self.b_rep = dram("b_rep", [128, HID], F32)
        self.normw_col = dram("normw_col", [128, 64], F32)
        self.convw_cols = dram("convw_cols", [128, 320], F32)
        self.convb_cols = dram("convb_cols", [128, 80], F32)
        self.dtb_col = dram("dtb_col", [128, 1], F32)
        self.alog_col = dram("alog_col", [128, 1], F32)
        self.d_bc_in = dram("d_bc_in", [128, 128], F32)
        self.prior_row = dram("prior_row", [1, 4], F32)
        self.rb_row = dram("rb_row", [1, 4], F32)
        self.halo_mask = dram("halo_mask", [4, 1], F32)
        self.selT_in = dram("selT_in", [8, 8], F32)
        self.rmask_in = dram("rmask_in", [8, 1], F32)
        self.ident_bf = dram("ident_bf", [128, 128], BF16)
        self.ident_f32 = dram("ident_f32", [128, 128], F32)
        self.tri01_rep = dram("tri01_rep", [128, 128], F32)
        self.trineg_bf = dram("trineg_bf", [128, 128], BF16)
        self.out_hbm = dram("out", [TOK, HID], F32, "ExternalOutput")

        self.proj_hbm = dram("proj_hbm", [NSTRIP * 128, TOKH], BF16, dbg)
        self.hs_tm_hbm = dram("hs_tm_hbm", [NZ * 128, INTER], BF16, dbg)
        self.s_hbm = dram("s_hbm", [NZ * 128, INTER], BF16, dbg)
        self.normed_hbm = dram("normed_hbm", [INTER, TOK], BF16, dbg)
        self.stream_hbm = dram("stream_hbm", [TOK, HID], BF16, dbg)
        self.lpr_d = dram("lpr_d", [1, 4], F32, "Internal")
        self.cc_a_in = dram("cc_a_in", [1, 128], F32, "Internal")
        self.cc_a_out = dram("cc_a_out", [8, 128], F32, "Internal")
        self.cc_s_in = dram("cc_s_in", [128, INTER], BF16, "Internal")
        self.cc_s_out = dram("cc_s_out", [8 * 128, INTER], BF16, "Internal")
        if self.debug:
            self.dbg_xfm = dram("dbg_xfm", [128, NCH * TOKH], BF16, "ExternalOutput")
            self.dbg_stream = dram("dbg_stream", [128, NT * HID + HID], BF16, "ExternalOutput")
            self.dbg_b = dram("dbg_b", [32, 2048], BF16, "ExternalOutput")
            self.dbg_a = dram("dbg_a", [128, 2048], F32, "ExternalOutput")

        sb = self.sb
        self.consts = sb("consts", [128, 8], F32)
        self.ones_bf = sb("ones_bf", [128, 128], BF16)
        self.id_bf = sb("id_bf", [128, 128], BF16)
        self.id_f32 = sb("id_f32", [128, 128], F32)
        self.dbc = sb("dbc", [128, 128], F32)
        self.hmask = sb("hmask", [4, 1], F32)
        self.tri_rep = sb("tri_rep", [128, 128], F32)
        self.trineg = sb("trineg", [128, 128], BF16)

        self.phase0()
        if self.n_phase >= 1:
            self.ln_block("a")
        if self.n_phase >= 2:
            self.ffn_block(first=True)
        if self.n_phase >= 3:
            self.ln2_spill()
        if self.n_phase >= 4:
            self.inproj_block()
        if self.n_phase >= 5:
            self.conv_block()
            self.states_block()
        if self.n_phase >= 6:
            self.ssd_block()
        if self.n_phase >= 7:
            self.outproj_block()
        if self.n_phase >= 8:
            self.ln3_block()
            self.ffn_block(first=False)
        self.final_block()
        for scope in reversed(list(self._scopes.keys())):
            self.close_scope(scope)
        self.ctx.close()
        return nc

    def phase0(self):
        self.stream = self.sb("stream", [128, NT * HID], BF16, scope="stream")
        self.halo = self.sb("halo", [4, HID], BF16, scope="stream")
        self.xfm = self.sb("xfm", [128, NCH * TOKH], BF16, scope="xfm", side="right")
        self.begin()
        t = self.t
        t("G", lambda g: g.memset(self.consts[:, 0:1], 1e-5))
        t("G", lambda g: g.memset(self.consts[:, 1:2], 1e-12))
        t("G", lambda g: g.memset(self.consts[:, 2:3], 1.0))
        t("G", lambda g: g.memset(self.consts[:, 7:8], 0.0))
        t("G", lambda g: g.memset(self.ones_bf[:, :], 1.0))
        src = _ap(self.x_in, 3 * HID, [[HID, 128], [128 * HID, NT], [1, HID]])
        s_ldg = self.sem("sldg0")
        s_ldy = self.sem("sldy0")
        t("G", lambda g: g.dma_start(self.stream[:, :], src), inc=s_ldg, inck=16, dma=True)
        t("G", lambda g: g.dma_start(self.halo[0:3, :], self.x_in[0:3, :]),
          inc=s_ldg, inck=16, dma=True)
        t("Y", lambda s: s.dma_start(self.id_bf[:, :], self.ident_bf),
          inc=s_ldy, inck=16)
        t("Y", lambda s: s.dma_start(self.id_f32[:, :], self.ident_f32),
          inc=s_ldy, inck=16)
        t("Y", lambda s: s.dma_start(self.dbc[:, :], self.d_bc_in),
          inc=s_ldy, inck=16)
        t("Y", lambda s: s.dma_start(self.hmask[:, :], self.halo_mask),
          inc=s_ldy, inck=16)
        t("Y", lambda s: s.dma_start(self.tri_rep[:, :], self.tri01_rep),
          inc=s_ldy, inck=16)
        t("Y", lambda s: s.dma_start(self.trineg[:, :], self.trineg_bf),
          inc=s_ldy, inck=16)
        t("Y", lambda s: None, waits=[(s_ldy, s_ldy.n), (s_ldg, s_ldg.n)])
        t("G", lambda g: None, waits=[(s_ldy, s_ldy.n), (s_ldg, s_ldg.n)], dma=True)
        self.flush("p0")

    def tile_tm(self, tt):
        return self.stream[:, tt * HID:(tt + 1) * HID] if tt < NT else self.halo[0:3, :]

    def xslice(self, d, tt, r):
        base = TOKH * d + (3 + 128 * tt if tt < NT else 0)
        return self.xfm[:, base:base + r]

    def ln_block(self, tag, src_tiles=None, ntile=9):
        sc = "ln" + tag
        stats = self.sb("stats" + tag, [128, 48], F32, scope=sc)
        mv = self.sb("mv" + tag, [128, 2 * 9], F32, scope=sc)
        rstd = self.sb("rstd" + tag, [128, 9], F32, scope=sc)
        lnout = self.sb("lnout" + tag, [128, 2 * HID], BF16, scope=sc)
        pb = [self.ps(f"pb{tag}{i}", [128, 1024], BF16, scope=sc) for i in range(2)]
        s_ag = self.sem("sag")
        s_rt = self.sem("srt")
        s_no = self.sem("sno")
        s_tp = self.sem("stp")
        s_ev = self.sem("sev")
        self.fresh([s_ag, s_rt, s_no, s_tp, s_ev])
        get = src_tiles if src_tiles is not None else self.tile_tm

        self.begin()
        t = self.t
        for tt in range(ntile):
            r = ROWS[tt]
            srct = get(tt)
            for j in range(8):
                t("V", (lambda v, r=r, j=j, srct=srct:
                        v.bn_stats(stats[0:r, 6 * j:6 * j + 6],
                                   srct[0:r, 512 * j:512 * j + 512])))
            t("V", (lambda v, r=r, tt=tt:
                    v.bn_aggr(mv[0:r, 2 * tt:2 * tt + 2], stats[0:r, 0:48])),
              inc=s_ag)
            t("S", (lambda s, r=r, tt=tt:
                    s.activation(rstd[0:r, tt:tt + 1], mv[0:r, 2 * tt + 1:2 * tt + 2],
                                 AF.Sqrt, bias=self.consts[0:r, 0:1], scale=1.0)),
              waits=[(s_ag, tt + 1)], inc=s_rt)
            t("V", (lambda v, r=r, tt=tt:
                    v.reciprocal(rstd[0:r, tt:tt + 1], rstd[0:r, tt:tt + 1])),
              waits=[(s_rt, tt + 1)])
            buf = (tt % 2) * HID
            t("V", (lambda v, r=r, tt=tt, srct=srct, buf=buf:
                    v.tensor_scalar(lnout[0:r, buf:buf + HID], srct[0:r, :],
                                    mv[0:r, 2 * tt:2 * tt + 1], rstd[0:r, tt:tt + 1],
                                    ALU.subtract, ALU.mult)),
              waits=([(s_tp, 4 * tt)] if tt >= 1 else []), inc=s_no)
            for grp in range(4):
                gidx = 4 * tt + grp
                w = [(s_no, tt + 1)]
                if gidx >= 2:
                    w.append((s_ev, gidx - 1))
                for j in range(8):
                    k = 8 * grp + j
                    t("T", (lambda te, r=r, grp=grp, j=j, k=k, buf=buf:
                            te.transpose(pb[grp % 2][:, 128 * j:128 * j + r],
                                         lnout[0:r, buf + 128 * k:buf + 128 * k + 128],
                                         self.id_bf[0:r, 0:r])),
                      waits=(w if j == 0 else []), inc=(s_tp if j == 7 else None))
                off = TOKH * 8 * grp + (3 + 128 * tt if tt < NT else 0)
                t("S", (lambda s, r=r, grp=grp, off=off:
                        s.copy(
                            _ap(self.xfm[:, :], off, [[NCH * TOKH, 128], [TOKH, 8], [1, r]]),
                            _ap(pb[grp % 2][:, :], 0, [[1024, 128], [128, 8], [1, r]]))),
                  waits=[(s_tp, gidx + 1)], inc=s_ev)
        self.flush("ln" + tag)
        self.close_scope(sc)

    def ffn_block(self, first):
        tag = "f1" if first else "f2"
        sc = "ffn" + tag
        import os as _os2
        ntile = (8 if _os2.environ.get("NTILE8") else 9) if first else 8
        sbl = lambda n, s, d: self.sb(n + tag, s, d, scope=sc)
        psl = lambda n, s, d: self.ps(n + tag, s, d, scope=sc)

        w36nat = sbl("w36nat", [36, HID], BF16)
        w36T = sbl("w36T", [128, NCH * 36], BF16)
        bcat_bf = sbl("bcat_bf", [32, HID], BF16)
        halfb = sbl("halfb", [128, HID], BF16)
        lpr = sbl("lpr", [1, 4], F32)
        rbs = sbl("rbs", [1, 4], F32)
        lpr_rep = sbl("lprrep", [128, 4], F32)
        zlog = sbl("zlog", [128, 9 * 64], F32)
        nmax = sbl("nmax", [128, 9], F32)
        ssum = sbl("ssum", [128, 9], F32)
        alpha = sbl("alpha", [128, 9 * 4], F32)
        zp = sbl("zp", [128, 2 * 32], BF16)
        zpfm = sbl("zpfm", [32, 9 * 128], BF16)
        wstrip = sbl("wstrip", [128, NCH * 512], BF16)
        t1ab = [sbl("t1a", [128, 512], F32), sbl("t1b", [128, 512], F32)]
        if not first:
            oxb = [sbl("oxa", [128, 512], F32), sbl("oxb", [128, 512], F32)]
            s_ox = [self.sem("sox0"), self.sem("sox1")]
            s_oe = self.sem("soe")
        pz = [psl("pza", [128, 512], F32), psl("pzb", [128, 512], F32)]
        pw = [psl(f"pw{i}", [128, 512], F32) for i in range(5)]
        pt = psl("pt", [128, 1024], BF16)

        s_wl = self.sem("swl" + tag)
        s_hl = [self.sem("shl0" + tag), self.sem("shl1" + tag)]
        s_pre = self.sem("spre")
        s_w36 = self.sem("sw36")
        s_zmm = self.sem("szmm")
        s_zc = self.sem("szc")
        s_nm = self.sem("snm")
        s_ex = self.sem("sex")
        s_al = self.sem("sal")
        s_zt = self.sem("szt")
        s_zp = self.sem("szp")
        s_mmw = self.sem("smmw")
        s_evw = self.sem("sevw")
        s_hu = self.sem("shu")
        s_wly = self.sem("swly")
        s_lprd = self.sem("slprd")
        allsems = [s_pre, s_w36, s_zmm, s_zc, s_nm,
                   s_ex, s_al, s_zt, s_zp, s_mmw, s_evw, s_hu, s_wly, s_lprd]
        if not first:
            allsems += [s_ox[0], s_ox[1], s_oe]
        self.fresh(allsems)

        self.begin()
        t = self.t
        t("G", lambda g: g.dma_start(w36nat[0:36, :], self.w36), inc=s_wl, inck=16, dma=True)
        t("G", lambda g: g.dma_start(bcat_bf[:, :], self.bcatT), inc=s_wl, inck=16, dma=True)
        t("Y", lambda s: s.dma_start(lpr[0:1, :], self.prior_row), inc=s_wly, inck=16)
        t("Y", lambda s: s.dma_start(rbs[0:1, :], self.rb_row), inc=s_wly, inck=16)
        t("G", lambda g: g.dma_start(halfb[:, :], self.b_rep), inc=s_wl, inck=16, dma=True)
        LD = s_wl.n
        LDY = s_wly.n
        import os as _os
        for og in ([] if _os.environ.get("SKIP_WB") else range(8)):
            for h in range(2):
                idx = 2 * og + h
                off = (2048 * h) * HID + 512 * og
                dims = [[HID, 128], [128 * HID, 16], [1, 512]]
                w = [(s_hu, idx // 2)] if idx >= 2 else []
                t("G", (lambda g, h=h, off=off, dims=dims:
                        g.dma_start(wstrip[:, 8192 * h:8192 * h + 8192],
                                    _ap(self.wb_T, off, dims))),
                  waits=w, inc=s_hl[h], inck=16, dma=True)
        t("V", lambda v: v.memset(zlog[:, :], 0.0))
        t("V", lambda v: v.memset(alpha[:, :], 0.0))
        t("V", lambda v: v.memset(ssum[:, :], 0.0))
        t("V", lambda v: v.memset(nmax[:, :], 0.0))
        t("V", lambda v: v.memset(zpfm[:, :], 0.0))
        t("V", lambda v: v.memset(zp[:, :], 0.0))
        t("S", lambda s: s.mul(halfb[:, :], halfb[:, :], 0.5),
          waits=[(s_wl, LD), (s_wly, LDY)], inc=s_pre)
        t("S", lambda s: s.activation(lpr[0:1, :], lpr[0:1, :], AF.Ln,
                                      bias=self.consts[0:1, 1:2], scale=1.0),
          inc=s_pre)
        t("V", lambda v: v.tensor_add(lpr[0:1, :], lpr[0:1, :], rbs[0:1, :]),
          waits=[(s_pre, 2)], inc=s_pre)
        t("Y", lambda s: s.dma_start(self.lpr_d, lpr[0:1, :]),
          waits=[(s_pre, 3)], inc=s_lprd, inck=16)
        t("Y", lambda s: s.dma_start(lpr_rep[:, :],
                                     _ap(self.lpr_d, 0, [[0, 128], [1, 4]])),
          waits=[(s_lprd, 16)], inc=s_lprd, inck=16)
        for d in range(NCH):
            t("T", (lambda te, d=d:
                    te.transpose(pt[:, 36 * (d % 8):36 * (d % 8) + 36],
                                 w36nat[0:36, 128 * d:128 * d + 128],
                                 self.id_bf[0:36, 0:36])),
              waits=([(s_wl, LD)] if d == 0 else []) +
                    ([(s_w36, 2 * (d // 8))] if d >= 8 and d % 8 == 0 else []),
              inc=(s_w36 if d % 8 == 7 else None))
            if d % 8 == 7:
                r0 = d // 8
                t("V", (lambda v, r0=r0:
                        v.tensor_copy(w36T[:, 36 * 8 * r0:36 * 8 * r0 + 36 * 8],
                                      pt[:, 0:36 * 8])),
                  waits=[(s_w36, 2 * r0 + 1)], inc=s_w36)
        W36R = s_w36.n
        import os
        SM = int(os.environ.get("SM_LEVEL", "6"))
        rr_ = lambda k: (ROWS[k] if first else 128)
        for tt in range(ntile + 3):
            if tt < ntile:
                r = rr_(tt)
                for d in range(NCH):
                    w = []
                    if d == 0:
                        w = [(s_w36, W36R)] + ([(s_zc, tt - 1)] if tt >= 2 else [])
                    t("T", (lambda te, tt=tt, r=r, d=d:
                            te.matmul(pz[tt % 2][0:r, 0:36], self.xslice(d, tt, r),
                                      w36T[:, 36 * d:36 * d + 36],
                                      start=(d == 0), stop=(d == NCH - 1))),
                      waits=w, inc=(s_zmm if d == NCH - 1 else None))
                t("V", (lambda v, tt=tt, r=r:
                        v.tensor_copy(zlog[0:r, 64 * tt:64 * tt + 36],
                                      pz[tt % 2][0:r, 0:36])),
                  waits=[(s_zmm, tt + 1)], inc=s_zc)
                t("V", (lambda v, tt=tt, r=r:
                        v.tensor_add(zlog[0:r, 64 * tt + 32:64 * tt + 36],
                                     zlog[0:r, 64 * tt + 32:64 * tt + 36],
                                     lpr_rep[0:r, :])),
                  waits=([(s_lprd, 32)] if tt == 0 else []))
                if SM < 2:
                    continue
                t("V", (lambda v, tt=tt, r=r:
                        v.reduce_max(nmax[0:r, tt:tt + 1],
                                     zlog[0:r, 64 * tt + 32:64 * tt + 36],
                                     axis=AX.X, negate=True)), inc=s_nm)
                t("S", (lambda s, tt=tt, r=r:
                        s.activation(alpha[0:r, 4 * tt:4 * tt + 4],
                                     zlog[0:r, 64 * tt + 32:64 * tt + 36], AF.Exp,
                                     bias=nmax[0:r, tt:tt + 1], scale=1.0)),
                  waits=[(s_zc, tt + 1), (s_nm, tt + 1)], inc=s_ex)
            k = tt - 1
            if SM >= 3 and 0 <= k < ntile:
                r = rr_(k)
                t("V", (lambda v, k=k, r=r:
                        v.reduce_sum(ssum[0:r, k:k + 1],
                                     alpha[0:r, 4 * k:4 * k + 4], axis=AX.X)),
                  waits=[(s_ex, k + 1)])
            k = tt - 2
            if SM >= 4 and 0 <= k < ntile:
                r = rr_(k)
                t("V", (lambda v, k=k, r=r:
                        v.reciprocal(ssum[0:r, k:k + 1], ssum[0:r, k:k + 1])))
            k = tt - 3
            if SM >= 5 and 0 <= k < ntile:
                r = rr_(k)
                t("V", (lambda v, k=k, r=r:
                        v.tensor_scalar(alpha[0:r, 4 * k:4 * k + 4],
                                        alpha[0:r, 4 * k:4 * k + 4],
                                        ssum[0:r, k:k + 1], 2.0,
                                        ALU.mult, ALU.mult)))
                t("V", (lambda v, k=k, r=r:
                        v.tensor_tensor(zp[0:r, 32 * (k % 2):32 * (k % 2) + 32],
                                        zlog[0:r, 64 * k:64 * k + 32],
                                        _ap(alpha[:, :], 4 * k, [[36, r], [1, 4], [0, 8]]),
                                        ALU.mult)),
                  waits=([(s_zt, k - 1)] if k >= 2 else []), inc=s_al)
                if SM < 6:
                    continue
                t("T", (lambda te, k=k, r=r:
                        te.transpose(pt[0:32, 128 * (k % 8):128 * (k % 8) + r],
                                     zp[0:r, 32 * (k % 2):32 * (k % 2) + 32],
                                     self.id_bf[0:r, 0:r])),
                  waits=[(s_al, k + 1)] + ([(s_zp, k - 7)] if k >= 8 else []),
                  inc=s_zt)
                t("S", (lambda s, k=k, r=r:
                        s.copy(zpfm[0:32, 128 * k:128 * k + r],
                               pt[0:32, 128 * (k % 8):128 * (k % 8) + r])),
                  waits=[(s_zt, k + 1)], inc=s_zp)
        NZP = s_zp.n
        import os
        if os.environ.get("SKIP_WB"):
            t("V", lambda v: None, waits=[(s_zp, NZP)], dma=True)
            self.ffn_done_sem = s_zp
            self.flush("ffn" + tag)
            self.close_scope(sc)
            return
        waves = [list(range(0, 5)), list(range(5, ntile))] if first \
            else [list(range(0, 4)), list(range(4, 8))]
        for og in range(8):
            for w_i, tl in enumerate(waves):
                widx = 2 * og + w_i
                for d in range(NCH):
                    h = d // 16
                    wts = []
                    if d % 16 == 0:
                        wts.append((s_hl[h], 16 * (og + 1)))
                        if d == 0 and widx >= 1:
                            if first:
                                wts.append((s_evw, widx))
                            else:
                                wts.append((s_oe, 4 * widx))
                    for i, tt in enumerate(tl):
                        r = ROWS[tt] if first else 128
                        last_rd = (w_i == 1 and d == NCH - 1 and i == len(tl) - 1)
                        t("T", (lambda te, i=i, tt=tt, r=r, d=d, h=h:
                                te.matmul(pw[i][0:r, :], self.xslice(d, tt, r),
                                          wstrip[:, 8192 * h + 512 * (d % 16):
                                                 8192 * h + 512 * (d % 16) + 512],
                                          start=(d == 0), stop=False)),
                          waits=(wts if i == 0 else []),
                          inc=(s_hu if last_rd else None))
                for i, tt in enumerate(tl):
                    r = ROWS[tt] if first else 128
                    t("T", (lambda te, i=i, tt=tt, r=r, og=og:
                            te.matmul(pw[i][0:r, :], zpfm[0:32, 128 * tt:128 * tt + r],
                                      bcat_bf[0:32, 512 * og:512 * og + 512],
                                      start=False, stop=True)),
                      waits=([(s_zp, NZP)] if i == 0 else []),
                      inc=(s_mmw if i == len(tl) - 1 else None))
                for i, tt in enumerate(tl):
                    r = ROWS[tt] if first else 128
                    t1 = t1ab[i % 2]
                    t("V", (lambda v, i=i, tt=tt, r=r, og=og, t1=t1:
                            v.scalar_tensor_tensor(t1[0:r, :], pw[i][0:r, :], 0.5,
                                                   halfb[0:r, 512 * og:512 * og + 512],
                                                   ALU.mult, ALU.add)),
                      waits=([(s_mmw, widx + 1)] if i == 0 else []))
                    if first:
                        dstf = (lambda tt, r, og:
                                (self.stream[0:r, HID * tt + 512 * og:HID * tt + 512 * og + 512]
                                 if tt < NT else self.halo[0:3, 512 * og:512 * og + 512]))
                        t("V", (lambda v, tt=tt, r=r, og=og, t1=t1, dstf=dstf:
                                v.tensor_tensor(dstf(tt, r, og), dstf(tt, r, og),
                                                t1[0:r, :], ALU.add)),
                          inc=(s_evw if i == len(tl) - 1 else None))
                    else:
                        cnt = 8 * og + tt
                        ep = cnt % 2
                        wx = [(s_ox[ep], 16 * ((cnt - 2) // 2 + 1))] if cnt >= 2 else []
                        t("V", (lambda v, tt=tt, og=og, t1=t1, ep=ep:
                                v.tensor_tensor(
                                    oxb[ep][:, :],
                                    self.stream2[:, HID * tt + 512 * og:
                                                 HID * tt + 512 * og + 512],
                                    t1[0:128, :], ALU.add)),
                          waits=wx, inc=s_oe)
                        t("Y", (lambda s, tt=tt, og=og, ep=ep:
                                s.dma_start(self.out_hbm[128 * tt:128 * tt + 128,
                                                         512 * og:512 * og + 512],
                                            oxb[ep][:, :])),
                          waits=[(s_oe, s_oe.n)], inc=s_ox[ep], inck=16, dma=True)
            if not first:
                pass
        if not first:
            pass
        if self.debug and first and SM >= 6:
            s_dbg = self.sem("sdbg")
            t("Y", lambda s: s.dma_start(self.dbg_a[:, 0:576], zlog[:, :]),
              waits=[(s_evw, s_evw.n)], inc=s_dbg, inck=16)
            t("Y", lambda s: s.dma_start(self.dbg_a[:, 576:612], alpha[:, :]),
              inc=s_dbg, inck=16)
            t("Y", lambda s: s.dma_start(self.dbg_b[0:32, 0:1152], zpfm[:, :]),
              inc=s_dbg, inck=16)
            t("Y", lambda s: s.dma_start(self.dbg_a[:, 612:616], lpr_rep[:, :]),
              inc=s_dbg, inck=16)
            t("Y", lambda s: s.dma_start(self.dbg_a[:, 616:625], ssum[:, :]),
              inc=s_dbg, inck=16)
            t("Y", lambda s: s.dma_start(self.dbg_a[:, 625:634], nmax[:, :]),
              inc=s_dbg, inck=16)
            t("Y", lambda s: None, waits=[(s_dbg, s_dbg.n)])
        if not first:
            self.drain([s_ox[0], s_ox[1]])
        self.drain([s_wl, s_wly, s_lprd, s_hl[0], s_hl[1]])
        self.ffn_done_sem = s_evw
        self.flush("ffn" + tag)
        self.close_scope(sc)

    def ln2_spill(self):
        self.begin()
        t = self.t
        t("V", lambda v: v.tensor_scalar(self.halo[0:3, :], self.halo[0:3, :],
                                         self.hmask[0:3, 0:1], None, ALU.mult))
        self.flush("mask")
        self.ln_block("b")
        s_sp = self.sem("sspill")
        self.begin()
        for tt in range(NT):
            self.t("Y", (lambda s, tt=tt:
                         s.dma_start(self.stream_hbm[128 * tt:128 * tt + 128, :],
                                     self.stream[:, HID * tt:HID * tt + HID])),
                   inc=s_sp, inck=16)
        self.t("Y", lambda s: None, waits=[(s_sp, s_sp.n)])
        self.t("G", lambda g: None, waits=[(s_sp, s_sp.n)], dma=True)
        self.flush("spill")
        self.close_scope("stream")

    def inproj_block(self):
        sc = "p4"
        self.acum = self.sb("acum", [128, TOK], F32, scope="dt")
        self.negacum = self.sb("negacum", [128, TOK], F32, scope="dt")
        self.lam_bc = self.sb("lam_bc", [128, TOK], F32, scope="dt")
        self.dt_tm = self.sb("dt_tm", [128, TOK], F32, scope="dt")
        self.expa_tm = self.sb("expa_tm", [128, TOK], F32, scope="dt")
        self.dtdec_tm = self.sb("dtdec_tm", [128, TOK], F32, scope="dt2")
        self.aseg = self.sb("aseg", [128, 1], F32, scope="dt2")
        ipstrip = self.sb("ipstrip", [128, 2 * HID], BF16, scope=sc)
        stage = self.sb("ipstage", [128, 2 * TOKH], BF16, scope=sc)
        pso = [self.ps(f"pso{i}", [128, 512], F32, scope=sc) for i in range(6)]
        dt_raw = self.sb("dt_raw", [128, TOKH], F32, scope=sc)
        dt_f = self.sb("dt_f", [128, TOK], F32, scope=sc)
        a_dt = self.sb("a_dt", [128, TOK], F32, scope=sc)
        acum_tm = self.sb("acum_tm", [128, TOK], F32, scope=sc)
        albc = self.sb("albc", [128, TOK], F32, scope=sc)
        tmp4 = self.sb("tmp4", [128, TOK], F32, scope=sc)
        dtb_sb = self.sb("dtb_sb", [128, 1], F32, scope=sc)
        alog_sb = self.sb("alog_sb", [128, 1], F32, scope=sc)
        al8 = self.sb("al8", [128, 8], F32, scope=sc)
        alT = self.sb("alT", [8, 128], F32, scope=sc)
        pdt = [self.ps(f"pdt{i}", [128, 512], F32, scope=sc) for i in range(2)]
        alP0_d = self.nc.dram_tensor("alP0_d", [1, TOK], F32).ap()

        s_ip = [self.sem("sip0"), self.sem("sip1")]
        s_ipu = self.sem("sipu")
        s_evV = self.sem("sevV4")
        s_evS = self.sem("sevS4")
        s_st = [self.sem("sst0"), self.sem("sst1")]
        s_dt = self.sem("sdt")
        s_dty = self.sem("sdty")
        s_dtg = self.sem("sdtg")

        self.begin()
        t = self.t
        t("Y", lambda s: s.dma_start(dtb_sb[:, :], self.dtb_col), inc=s_dty, inck=16, dma=True)
        t("Y", lambda s: s.dma_start(alog_sb[:, :], self.alog_col), inc=s_dty, inck=16, dma=True)
        for s_ in range(NSTRIP):
            par = s_ % 2
            w = [(s_ipu, s_ - 1)] if s_ >= 2 else []
            t("G", (lambda g, s_=s_, par=par:
                    g.dma_start(ipstrip[:, HID * par:HID * par + HID],
                                _ap(self.ip_T, 128 * s_,
                                    [[PROJ, 128], [128 * PROJ, 32], [1, 128]]))),
              waits=w, inc=s_ip[par], inck=16, dma=True)
        for s_ in range(NSTRIP):
            par = s_ % 2
            for gi, (t0, tl) in enumerate(TGS):
                bank = pso[3 * par + gi]
                for d in range(NCH):
                    w = []
                    if d == 0 and gi == 0:
                        w = [(s_ip[par], 16 * (s_ // 2 + 1))]
                        if s_ >= 2:
                            w += [(s_evV, 2 * (s_ - 1)), (s_evS, s_ - 1)]
                    t("T", (lambda te, s_=s_, par=par, gi=gi, t0=t0, tl=tl, d=d, bank=bank:
                            te.matmul(bank[:, 0:tl],
                                      ipstrip[:, HID * par + 128 * d:HID * par + 128 * d + 128],
                                      self.xfm[:, TOKH * d + t0:TOKH * d + t0 + tl],
                                      start=(d == 0), stop=(d == NCH - 1))),
                      waits=w,
                      inc=(s_ipu if (gi == 2 and d == NCH - 1) else None))
            wst = [(s_st[par], 16 * (s_ // 2))] if s_ >= 2 else []
            t("V", (lambda v, s_=s_, par=par:
                    v.tensor_copy(stage[:, TOKH * par:TOKH * par + 384],
                                  pso[3 * par][:, 0:384])),
              waits=[(s_ipu, s_ + 1)] + wst, inc=s_evV)
            t("S", (lambda s, s_=s_, par=par:
                    s.copy(stage[:, TOKH * par + 384:TOKH * par + 768],
                           pso[3 * par + 1][:, 0:384])),
              waits=[(s_ipu, s_ + 1)] + wst, inc=s_evS)
            t("V", (lambda v, s_=s_, par=par:
                    v.tensor_copy(stage[:, TOKH * par + 768:TOKH * par + 1027],
                                  pso[3 * par + 2][:, 0:259])),
              inc=s_evV)
            t("Y", (lambda s, s_=s_, par=par:
                    s.dma_start(self.proj_hbm[128 * s_:128 * s_ + 128, :],
                                stage[:, TOKH * par:TOKH * par + TOKH])),
              waits=[(s_evV, 2 * s_ + 2), (s_evS, s_ + 1)],
              inc=s_st[par], inck=16, dma=True)
            if s_ == NSTRIP - 1:
                t("V", (lambda v, par=par:
                        v.tensor_copy(dt_raw[:, 0:384], pso[3 * par][:, 0:384])),
                  inc=s_evV)
                t("V", (lambda v, par=par:
                        v.tensor_copy(dt_raw[:, 384:768], pso[3 * par + 1][:, 0:384])),
                  inc=s_evV)
                t("V", (lambda v, par=par:
                        v.tensor_copy(dt_raw[:, 768:1027], pso[3 * par + 2][:, 0:259])),
                  inc=s_evV)
        DTE = s_evV.n
        t("V", lambda v: v.tensor_scalar(dt_raw[:, 3:TOKH], dt_raw[:, 3:TOKH],
                                         dtb_sb[:, 0:1], None, ALU.add),
          waits=[(s_evV, DTE), (s_dty, 32)], inc=s_dt)
        vU = s_dt.n
        t("S", lambda s: s.activation(tmp4[:, :], dt_raw[:, 3:TOKH], AF.Abs,
                                      bias=self.consts[:, 7:8], scale=1.0),
          waits=[(s_dt, vU)])
        t("S", lambda s: s.activation(tmp4[:, :], tmp4[:, :], AF.Exp,
                                      bias=self.consts[:, 7:8], scale=-1.0))
        t("S", lambda s: s.activation(tmp4[:, :], tmp4[:, :], AF.Ln,
                                      bias=self.consts[:, 2:3], scale=1.0))
        t("S", lambda s: s.activation(dt_f[:, :], dt_raw[:, 3:TOKH], AF.Relu,
                                      bias=self.consts[:, 7:8], scale=1.0), inc=s_dt)
        sRelu = s_dt.n
        t("V", lambda v: v.tensor_add(dt_f[:, :], dt_f[:, :], tmp4[:, :]),
          waits=[(s_dt, sRelu)])
        t("V", lambda v: v.tensor_scalar(dt_f[:, :], dt_f[:, :], 100.0, None,
                                         ALU.min))
        t("S", lambda s: s.activation(al8[:, 0:1], alog_sb[:, :], AF.Exp,
                                      bias=self.consts[:, 7:8], scale=1.0), inc=s_dt)
        sAE = s_dt.n
        t("V", lambda v: v.tensor_scalar(al8[:, 1:2], al8[:, 0:1], -1.0, None,
                                         ALU.mult), waits=[(s_dt, sAE)])
        t("V", lambda v: v.tensor_scalar(a_dt[:, :], dt_f[:, :], al8[:, 1:2], None,
                                         ALU.mult))
        t("V", lambda v: v.reduce_sum(self.aseg[:, :], a_dt[:, :], axis=AX.X))
        for z in range(NZ):
            t("V", (lambda v, z=z:
                    v.tensor_tensor_scan(self.acum[:, 128 * z:128 * z + 128],
                                         a_dt[:, 128 * z:128 * z + 128],
                                         a_dt[:, 128 * z:128 * z + 128], 0.0,
                                         ALU.add, ALU.bypass)))
        t("V", lambda v: v.tensor_scalar(self.negacum[:, :], self.acum[:, :], -1.0,
                                         None, ALU.mult))
        t("V", lambda v: v.tensor_copy(al8[:, 0:8],
                                       _ap(self.acum[:, :], 127, [[TOK, 128], [128, 8]])),
          inc=s_dt)
        vAL = s_dt.n
        t("T", lambda te: te.transpose(pdt[1][0:8, 0:128], al8[:, 0:8],
                                       self.id_f32[:, :]),
          waits=[(s_dt, vAL)], inc=s_dt)
        tAL = s_dt.n
        t("V", lambda v: v.tensor_copy(alT[0:8, :], pdt[1][0:8, 0:128]),
          waits=[(s_dt, tAL)], inc=s_dt)
        vALT = s_dt.n
        t("G", lambda g: g.dma_start(alP0_d, alT[0:8, :]),
          waits=[(s_dt, vALT)], inc=s_dtg, inck=16, dma=True)
        t("G", lambda g: g.dma_start(albc[:, :], _ap(alP0_d, 0, [[0, 128], [1, TOK]])),
          waits=[(s_dtg, 16)], inc=s_dtg, inck=16, dma=True)
        t("S", lambda s: s.activation(self.lam_bc[:, :], albc[:, :], AF.Exp,
                                      bias=self.consts[:, 7:8], scale=1.0),
          waits=[(s_dtg, 32)])
        evdone = {}
        for j, (srcb, dstb) in enumerate([(self.acum, acum_tm), (dt_f, self.dt_tm)]):
            for z in range(NZ):
                i = j * NZ + z
                wt = [(s_dt, vALT)] if i == 0 else [(s_dt, evdone[i - 1])]
                t("T", (lambda te, srcb=srcb, z=z, i=i:
                        te.transpose(pdt[i % 2][0:128, 0:128],
                                     srcb[:, 128 * z:128 * z + 128],
                                     self.id_f32[:, :])),
                  waits=wt, inc=s_dt)
                tv = s_dt.n
                t("V", (lambda v, dstb=dstb, z=z, i=i:
                        v.tensor_copy(dstb[:, 128 * z:128 * z + 128],
                                      pdt[i % 2][0:128, 0:128])),
                  waits=[(s_dt, tv)], inc=s_dt)
                evdone[i] = s_dt.n
        t("V", lambda v: v.tensor_sub(tmp4[:, :], albc[:, :], acum_tm[:, :]),
          waits=[(s_dtg, 32)], inc=s_dt)
        vS = s_dt.n
        t("S", lambda s: s.activation(tmp4[:, :], tmp4[:, :], AF.Exp,
                                      bias=self.consts[:, 7:8], scale=1.0),
          waits=[(s_dt, vS)], inc=s_dt)
        sE = s_dt.n
        t("V", lambda v: v.tensor_mul(self.dtdec_tm[:, :], self.dt_tm[:, :],
                                      tmp4[:, :]), waits=[(s_dt, sE)])
        t("S", lambda s: s.activation(self.expa_tm[:, :], acum_tm[:, :], AF.Exp,
                                      bias=self.consts[:, 7:8], scale=1.0))
        self.drain([s_st[0], s_st[1], s_ip[0], s_ip[1], s_dtg, s_dty])
        self.flush("p4")
        self.close_scope(sc)

    def conv_block(self):
        self.close_scope("xfm")
        self.bc_post = self.sb("bc_post", [128, 16 * TOK], BF16, scope="bc", side="right")
        sc = "p5a"
        pre = self.sb("cvpre", [128, 2 * TOKH], BF16, scope=sc)
        t1 = self.sb("cvt1", [128, TOK], F32, scope=sc)
        hstile = self.sb("hstile", [128, 2 * TOK], BF16, scope=sc)
        hstg = self.sb("hstg", [128, 2 * TOK], BF16, scope=sc)
        convw_sb = self.sb("convw_sb", [128, 320], F32, scope=sc)
        convb_sb = self.sb("convb_sb", [128, 80], F32, scope=sc)
        pcv = [self.ps(f"pcv{i}", [128, 1024], BF16, scope=sc) for i in range(2)]
        s_cl = [self.sem("scl0"), self.sem("scl1")]
        s_cu = [self.sem("scu0"), self.sem("scu1")]
        s_cw = self.sem("scw")
        s_cs = self.sem("scs")
        s_ct = self.sem("sct")
        s_ce = self.sem("sce")
        s_ch = [self.sem("sch0"), self.sem("sch1")]

        self.begin()
        t = self.t
        t("Y", lambda s: s.dma_start(convw_sb[:, :], self.convw_cols),
          inc=s_cw, inck=16, dma=True)
        t("Y", lambda s: s.dma_start(convb_sb[:, :], self.convb_cols),
          inc=s_cw, inck=16, dma=True)
        for si in range(80):
            par = si % 2
            gs = 64 + si
            w = [(s_cu[par], si // 2)] if si >= 2 else []
            t("Y", (lambda s, si=si, par=par, gs=gs:
                    s.dma_start(pre[:, TOKH * par:TOKH * par + TOKH],
                                self.proj_hbm[128 * gs:128 * gs + 128, :])),
              waits=w, inc=s_cl[par], inck=16, dma=True)
            t("V", (lambda v, si=si, par=par:
                    v.tensor_scalar(t1[:, :], pre[:, TOKH * par:TOKH * par + TOK],
                                    convw_sb[:, 4 * si:4 * si + 1], None, ALU.mult)),
              waits=[(s_cl[par], 16 * (si // 2 + 1)), (s_cw, 32)] +
                    ([(s_cs, si)] if si >= 1 else []))
            for k in range(1, 4):
                t("V", (lambda v, si=si, par=par, k=k:
                        v.scalar_tensor_tensor(
                            t1[:, :], pre[:, TOKH * par + k:TOKH * par + k + TOK],
                            convw_sb[:, 4 * si + k:4 * si + k + 1], t1[:, :],
                            ALU.mult, ALU.add)),
                  inc=(s_cu[par] if k == 3 else None))
            vT1s, vT1 = s_cu[par], s_cu[par].n
            if si < 64:
                hpar = si % 2
                t("S", (lambda s, si=si, hpar=hpar:
                        s.activation(hstile[:, TOK * hpar:TOK * hpar + TOK], t1[:, :],
                                     AF.Silu, bias=convb_sb[:, si:si + 1], scale=1.0)),
                  waits=[(vT1s, vT1)] +
                        ([(s_ct, si - 1)] if si >= 2 else []), inc=s_cs)
                for z in range(NZ):
                    t("T", (lambda te, si=si, hpar=hpar, z=z:
                            te.transpose(pcv[hpar][:, 128 * z:128 * z + 128],
                                         hstile[:, TOK * hpar + 128 * z:TOK * hpar + 128 * z + 128],
                                         self.id_bf[:, :])),
                      waits=([(s_cs, si + 1)] if z == 0 else []) +
                            ([(s_ce, si - 1)] if (z == 0 and si >= 2) else []),
                      inc=(s_ct if z == NZ - 1 else None))
                t("V", (lambda v, si=si, hpar=hpar:
                        v.tensor_copy(hstg[:, TOK * hpar:TOK * hpar + TOK],
                                      pcv[hpar][:, :])),
                  waits=[(s_ct, si + 1)] + ([(s_ch[hpar], 16 * (si // 2))] if si >= 2 else []),
                  inc=s_ce)
                t("Y", (lambda s, si=si, hpar=hpar:
                        s.dma_start(_ap(self.hs_tm_hbm, 128 * si,
                                        [[INTER, 128], [128 * INTER, 8], [1, 128]]),
                                    _ap(hstg[:, :], TOK * hpar,
                                        [[2 * TOK, 128], [128, 8], [1, 128]]))),
                  waits=[(s_ce, si + 1)], inc=s_ch[hpar], inck=16, dma=True)
            else:
                t("S", (lambda s, si=si:
                        s.activation(self.bc_post[:, TOK * (si - 64):TOK * (si - 64) + TOK],
                                     t1[:, :], AF.Silu,
                                     bias=convb_sb[:, si:si + 1], scale=1.0)),
                  waits=[(vT1s, vT1)], inc=s_cs)
        self.drain([s_ch[0], s_ch[1], s_cl[0], s_cl[1], s_cw])
        self.flush("p5a")
        self.close_scope(sc)

    def states_block(self):
        self.prev_bf = self.sb("prev_bf", [128, INTER], BF16, scope="prev", side="right")
        sc = "p5b"
        srun = self.sb("srun", [128, INTER], F32, scope=sc)
        hstm = self.sb("hstm5", [128, 2 * INTER], BF16, scope=sc)
        szst = self.sb("szst", [128, 2 * INTER], BF16, scope=sc)
        btmz = self.sb("btmz", [128, 1024], BF16, scope=sc)
        xdec = self.sb("xdec5", [128, 2 * 512], BF16, scope=sc)
        wsb = self.sb("wsb", [8, 128], F32, scope=sc)
        wbfr = self.sb("wbfr", [8, 128], BF16, scope=sc)
        ag_sb = self.sb("ag_sb", [8, 128], F32, scope=sc)
        selT_sb = self.sb("selT_sb", [8, 8], F32, scope=sc)
        rmask_sb = self.sb("rmask_sb", [8, 1], F32, scope=sc)
        pbt = self.ps("pbt", [128, 1024], BF16, scope=sc)
        pst = [self.ps(f"pst{i}", [128, 512], F32, scope=sc) for i in range(2)]
        pw8 = self.ps("pw8", [128, 512], F32, scope=sc)
        wexp = self.sb("wexp", [8, INTER], BF16, scope=sc)
        wexp_d = self.nc.dram_tensor("wexp_d", [8, INTER], BF16).ap()

        s_hl5 = [self.sem("shl5a"), self.sem("shl5b")]
        s_bt = self.sem("sbt")
        s_sm = self.sem("ssm")
        s_sv = self.sem("ssv")
        s_sz = [self.sem("ssz0"), self.sem("ssz1")]
        s_cc = self.sem("scc")
        s_cg = self.sem("scg")
        s_cy = self.sem("scy")
        s_ck = [self.sem("sck0"), self.sem("sck1")]
        SV5 = {}
        s_wk = [self.sem("swk0"), self.sem("swk1")]

        self.begin()
        t = self.t
        t("V", lambda v: v.memset(srun[:, :], 0.0))
        t("Y", lambda s: s.dma_start(selT_sb[:, :], self.selT_in),
          inc=s_cy, inck=16, dma=True)
        t("Y", lambda s: s.dma_start(rmask_sb[:, :], self.rmask_in),
          inc=s_cy, inck=16, dma=True)
        for z in range(NZ):
            par = z % 2
            t("Y", (lambda s, z=z, par=par:
                    s.dma_start(hstm[:, INTER * par:INTER * par + INTER],
                                self.hs_tm_hbm[128 * z:128 * z + 128, :])),
              waits=([(s_sm, 16 * (z - 1))] if z >= 2 else []),
              inc=s_hl5[par], inck=16, dma=True)
            for g_ in range(8):
                t("T", (lambda te, z=z, g_=g_:
                        te.transpose(pbt[:, 128 * g_:128 * g_ + 128],
                                     self.bc_post[:, 1024 * g_ + 128 * z:1024 * g_ + 128 * z + 128],
                                     self.id_bf[:, :])),
                  waits=([(s_bt, 2 * z)] if z >= 1 else []),
                  inc=(s_bt if g_ == 7 else None))
            t("V", (lambda v, z=z:
                    v.tensor_copy(btmz[:, :], pbt[:, :])),
              waits=[(s_bt, 2 * z + 1)], inc=s_bt)
            for hb in range(16):
                bpar = hb % 2
                t("V", (lambda v, z=z, par=par, hb=hb, bpar=bpar:
                        v.tensor_tensor(xdec[:, 512 * bpar:512 * bpar + 512],
                                        hstm[:, INTER * par + 512 * hb:INTER * par + 512 * hb + 512],
                                        _ap(self.dtdec_tm, 128 * z + 8 * hb,
                                            [[TOK, 128], [1, 8], [0, 64]]),
                                        ALU.mult)),
                  waits=([(s_hl5[par], 16 * (z // 2 + 1)), (s_bt, 2 * z + 2)]
                         if hb == 0 else []) +
                        ([(s_sv, SV5[(z, hb - 2)] if hb >= 2 else SV5[(z - 1, 14 + hb)])]
                         if (16 * z + hb) >= 2 else []))
                for j in range(8):
                    h = 8 * hb + j
                    g_ = h // 16
                    t("T", (lambda te, z=z, h=h, j=j, g_=g_, bpar=bpar:
                            te.matmul(pst[bpar][:, 64 * j:64 * j + 64],
                                      btmz[:, 128 * g_:128 * g_ + 128],
                                      xdec[:, 512 * bpar + 64 * j:512 * bpar + 64 * j + 64],
                                      start=True, stop=True)),
                      waits=(([(self._seq["V"], self._seq["V"].n)] +
                              ([(s_sv, SV5[(z, hb - 2)] if hb >= 2 else SV5[(z - 1, 14 + hb)])]
                               if (16 * z + hb) >= 2 else [])) if j == 0 else []),
                      inc=(s_sm if j == 7 else None))
                bi = 16 * z + hb
                t("V", (lambda v, z=z, hb=hb, par=par, bpar=bpar:
                        v.tensor_copy(szst[:, INTER * (z % 2) + 512 * hb:INTER * (z % 2) + 512 * hb + 512],
                                      pst[bpar][:, :])),
                  waits=[(s_sm, bi + 1)] +
                        ([(s_sz[z % 2], 16 * (z // 2))] if (hb == 0 and z >= 2) else []),
                  inc=s_sv)
                t("V", (lambda v, z=z, hb=hb:
                        v.tensor_tensor(srun[:, 512 * hb:512 * hb + 512],
                                        srun[:, 512 * hb:512 * hb + 512],
                                        _ap(self.lam_bc, 128 * z + 8 * hb,
                                            [[TOK, 128], [1, 8], [0, 64]]),
                                        ALU.mult)))
                t("V", (lambda v, hb=hb, bpar=bpar:
                        v.tensor_add(srun[:, 512 * hb:512 * hb + 512],
                                     srun[:, 512 * hb:512 * hb + 512],
                                     pst[bpar][:, :])),
                  inc=s_sv)
                SV5[(z, hb)] = s_sv.n
            t("Y", (lambda s, z=z:
                    s.dma_start(self.s_hbm[128 * z:128 * z + 128, :],
                                szst[:, INTER * (z % 2):INTER * (z % 2) + INTER])),
              waits=[(s_sv, SV5[(z, 15)])], inc=s_sz[z % 2], inck=16, dma=True)
        t("V", lambda v: v.tensor_copy(hstm[:, 0:INTER], srun[:, :]), inc=s_sv)
        vSEND = s_sv.n
        t("G", lambda g: g.dma_start(self.cc_s_in, hstm[:, 0:INTER]),
          waits=[(s_sv, vSEND)], inc=s_cg, inck=16, dma=True)
        t("G", lambda g: g.dma_start(self.cc_a_in,
                                     _ap(self.aseg[:, :], 0, [[1, 128], [1, 1]])),
          waits=[(s_cg, 16)], inc=s_cg, inck=16, dma=True)
        t("G", lambda g: g.collective_compute(
            "AllGather", ALU.bypass, replica_groups=[list(range(8))],
            ins=[self.cc_a_in], outs=[self.cc_a_out]),
          waits=[(s_cg, 32)], inc=s_cc, dma=True)
        t("G", lambda g: g.collective_compute(
            "AllGather", ALU.bypass, replica_groups=[list(range(8))],
            ins=[self.cc_s_in], outs=[self.cc_s_out]),
          inc=s_cc, dma=True)
        t("Y", lambda s: s.dma_start(ag_sb[:, :], self.cc_a_out),
          waits=[(s_cc, 2), (s_cy, 32)], inc=s_cy, inck=16, dma=True)
        t("T", lambda te: te.matmul(pw8[0:8, 0:128], selT_sb[0:8, 0:8],
                                    ag_sb[0:8, 0:128], start=True, stop=True),
          waits=[(s_cy, 48)], inc=s_sm)
        eMM = s_sm.n
        t("S", lambda s: s.activation(wsb[0:8, :], pw8[0:8, 0:128], AF.Exp,
                                      bias=self.consts[0:8, 7:8], scale=1.0),
          waits=[(s_sm, eMM)], inc=s_sv)
        t("V", lambda v: v.tensor_scalar(wsb[0:8, :], wsb[0:8, :],
                                         rmask_sb[0:8, 0:1], None, ALU.mult),
          waits=[(s_sv, s_sv.n)])
        t("V", lambda v: v.tensor_copy(wbfr[0:8, :], wsb[0:8, :]))
        t("V", lambda v: v.tensor_copy(wexp[0:8, :],
                                       _ap(wbfr, 0, [[128, 8], [1, 128], [0, 64]])),
          inc=s_sv)
        vWB = s_sv.n
        t("G", lambda g: g.dma_start(wexp_d, wexp[0:8, :]),
          waits=[(s_sv, vWB)], inc=s_cg, inck=16, dma=True)
        SZF = [s_sz[0].n, s_sz[1].n]
        for k in range(8):
            par = k % 2
            wfree = ([(s_sv, vWB + k - 1)] if k >= 2 else
                     [(s_cc, 2), (s_cg, 48), (s_sz[0], SZF[0]), (s_sz[1], SZF[1])])
            t("G", (lambda g, k=k, par=par:
                    g.dma_start(szst[:, INTER * par:INTER * par + INTER],
                                _ap(wexp_d, INTER * k, [[0, 128], [1, INTER]]))),
              waits=wfree, inc=s_wk[par], inck=16, dma=True)
            t("G", (lambda g, k=k, par=par:
                    g.dma_start(hstm[:, INTER * par:INTER * par + INTER],
                                self.cc_s_out[128 * k:128 * k + 128, :])),
              inc=s_ck[par], inck=16, dma=True)
            if k == 0:
                t("V", (lambda v, par=par:
                        v.tensor_tensor(srun[:, :], hstm[:, INTER * par:INTER * par + INTER],
                                        szst[:, INTER * par:INTER * par + INTER], ALU.mult)),
                  waits=[(s_wk[par], 16), (s_ck[par], 16)], inc=s_sv)
            else:
                t("V", (lambda v, k=k, par=par:
                        v.tensor_tensor(szst[:, INTER * par:INTER * par + INTER],
                                        hstm[:, INTER * par:INTER * par + INTER],
                                        szst[:, INTER * par:INTER * par + INTER], ALU.mult)),
                  waits=[(s_wk[par], 16 * (k // 2 + 1)), (s_ck[par], 16 * (k // 2 + 1))])
                t("V", (lambda v, par=par:
                        v.tensor_add(srun[:, :], srun[:, :],
                                     szst[:, INTER * par:INTER * par + INTER])),
                  inc=s_sv)
        t("V", lambda v: v.tensor_copy(self.prev_bf[:, :], srun[:, :]))
        self.drain([s_hl5[0], s_hl5[1], s_sz[0], s_sz[1], s_ck[0], s_ck[1],
                    s_wk[0], s_wk[1], s_cg, s_cy])
        self.flush("p5b")
        self.close_scope(sc)

    def ssd_block(self):
        sc = "p6"
        self.close_scope("dt2")
        stk = [self.sb(f"stk{i}", [2, 2048], F32, scope=sc) for i in range(2)]
        hstm = self.sb("hstm6", [128, 2 * INTER], BF16, scope=sc)
        sz6 = self.sb("sz6", [128, INTER], BF16, scope=sc)
        gfm = self.sb("gfm", [128, INTER], BF16, scope=sc)
        stm = self.sb("stm", [128, INTER], BF16, scope=sc)
        ytm = self.sb("ytm", [128, INTER], BF16, scope=sc)
        mtri = self.sb("mtri", [128, 1024], F32, scope=sc)
        lbf = self.sb("lbf", [128, 8 * 128], BF16, scope=sc)
        m2b = self.sb("m2b", [128, 8 * 128], BF16, scope=sc)
        xtm6 = self.sb("xtm6", [128, 2 * 512], BF16, scope=sc)
        t1y = self.sb("t1y", [128, 2 * 512], F32, scope=sc)
        vacc = self.sb("vacc", [128, 8], F32, scope=sc)
        rstd = self.sb("rstd6", [128, 8], F32, scope=sc)
        nw_col = self.sb("nw_col", [128, 64], F32, scope=sc)
        nstg = self.sb("nstg", [128, 2 * 1024], BF16, scope=sc)
        pg = [self.ps(f"pg6{i}", [128, 1024], BF16, scope=sc) for i in range(2)]
        pe6 = [self.ps(f"pe6{i}", [128, 512], F32, scope=sc) for i in range(2)]
        pyd = [self.ps(f"pyd{i}", [128, 512], F32, scope=sc) for i in range(2)]
        pyo = [self.ps(f"pyo{i}", [128, 512], F32, scope=sc) for i in range(2)]

        s_lh = [self.sem("slh0"), self.sem("slh1")]
        s_lz = self.sem("slz")
        s_lg = self.sem("slg")
        s_lk = [self.sem("slk0"), self.sem("slk1")]
        s_nw = self.sem("snw")
        s_gt = self.sem("sgt")
        s_ge = self.sem("sge")
        s_sl = self.sem("ssl")
        s_mr = self.sem("smr")
        s_mt = self.sem("smt")
        s_em = self.sem("sem6")
        s_ex6 = self.sem("sex6")
        s_m2 = self.sem("sm2")
        s_ym = self.sem("sym")
        s_yv = self.sem("syv")
        s_yg = self.sem("syg")
        s_xt = self.sem("sxt")
        s_t1 = self.sem("st1")
        s_gg = self.sem("sgg")
        s_sq = self.sem("ssq")
        s_ne = self.sem("sne")
        s_ns = [self.sem("sns0"), self.sem("sns1")]

        EM = {}; EXh = {}; M2d = {}; XT = {}
        EV = {}; GE = {}; T1 = {}; SQ = {}; GG = {}; NT = {}; NE = {}

        self.begin()
        t = self.t
        t("Y", lambda s: s.dma_start(nw_col[:, :], self.normw_col),
          inc=s_nw, inck=16, dma=True)
        t("V", lambda v: v.memset(stk[0][0:2, :], 1.0))
        t("V", lambda v: v.memset(stk[1][0:2, :], 1.0), inc=s_gg)
        MS0 = s_gg.n

        def stack_load(z, hb):
            gpar = hb % 2
            if hb >= 2:
                w = [(s_em, EM[(z, 8 * (hb - 1) - 1)])]
            elif z >= 1:
                w = [(s_em, EM[(z - 1, 8 * (hb + 15) - 1)])]
            else:
                w = [(s_gg, MS0)]
            t("Y", (lambda s, z=z, hb=hb, gpar=gpar:
                    s.dma_start(stk[gpar][1:2, 0:1024],
                                _ap(self.negacum, 8 * hb * TOK + 128 * z,
                                    [[TOK, 8], [1, 128]]))),
              waits=w, inc=s_lk[gpar], inck=16, dma=True)
            t("Y", (lambda s, z=z, hb=hb, gpar=gpar:
                    s.dma_start(stk[gpar][0:1, 1024:2048],
                                _ap(self.acum, 8 * hb * TOK + 128 * z,
                                    [[TOK, 8], [1, 128]]))),
              inc=s_lk[gpar], inck=16, dma=True)

        for z in range(NZ):
            par = z % 2
            wl = []
            if z >= 2:
                wl = [(s_yg, GE[(z - 2, 15)]), (s_sq, SQ[z - 2])]
            t("Y", (lambda s, z=z, par=par:
                    s.dma_start(hstm[:, INTER * par:INTER * par + INTER],
                                self.hs_tm_hbm[128 * z:128 * z + 128, :])),
              waits=wl, inc=s_lh[par], inck=16, dma=True)
            t("Y", (lambda s, z=z:
                    s.dma_start(sz6[:, :], self.s_hbm[128 * z:128 * z + 128, :])),
              waits=([(s_yv, EV[(z - 1, 15)])] if z >= 1 else []),
              inc=s_lz, inck=16, dma=True)
            t("Y", (lambda s, z=z:
                    s.dma_start(gfm[:, :],
                                _ap(self.proj_hbm, 128 * z + 3,
                                    [[TOKH, 128], [128 * TOKH, 64], [1, 128]]))),
              waits=([(s_gt, NT[z - 1])] if z >= 1 else []),
              inc=s_lg, inck=16, dma=True)
            stack_load(z, 0)
            stack_load(z, 1)
            t("S", (lambda s: s.activation(gfm[:, :], gfm[:, :], AF.Silu,
                                           bias=self.consts[:, 7:8], scale=1.0)),
              waits=[(s_lg, 16 * (z + 1))], inc=s_sl)
            for r_ in range(8):
                for j in range(8):
                    i = 8 * r_ + j
                    w = []
                    if j == 0:
                        w = [(s_sl, z + 1)]
                        if r_ >= 2:
                            w.append((s_ge, s_ge.n - 1))
                        elif z >= 1:
                            w.append((s_ne, NE[z - 1]))
                    t("T", (lambda te, i=i, r_=r_, j=j:
                            te.transpose(pg[r_ % 2][:, 128 * j:128 * j + 128],
                                         gfm[:, 128 * i:128 * i + 128],
                                         self.id_bf[:, :])),
                      waits=w, inc=(s_gt if j == 7 else None))
                t("V", (lambda v, r_=r_:
                        v.tensor_copy(stm[:, 1024 * r_:1024 * r_ + 1024],
                                      pg[r_ % 2][:, :])),
                  waits=[(s_gt, s_gt.n)], inc=s_ge)
            for g_ in range(8):
                w = []
                if g_ == 0:
                    w = [(s_ym, 16 * z)]
                    if z >= 1:
                        w.append((s_ex6, EXh[(z - 1, 127)]))
                if g_ >= 2:
                    w.append((s_mt, 8 * z + g_ - 1))
                t("T", (lambda te, z=z, g_=g_:
                        te.matmul(pe6[g_ % 2][:, 0:128],
                                  self.bc_post[:, 1024 * g_ + 128 * z:1024 * g_ + 128 * z + 128],
                                  self.bc_post[:, 1024 * (8 + g_) + 128 * z:1024 * (8 + g_) + 128 * z + 128],
                                  start=True, stop=True)),
                  waits=w, inc=s_mr)
                t("V", (lambda v, g_=g_:
                        v.scalar_tensor_tensor(mtri[:, 128 * g_:128 * g_ + 128],
                                               pe6[g_ % 2][:, 0:128], 1.0,
                                               self.tri_rep[:, 0:128],
                                               ALU.mult, ALU.mult)),
                  waits=[(s_mr, 8 * z + g_ + 1)], inc=s_mt)
            for hb in range(16):
                bpar = hb % 2
                if hb >= 2:
                    prev_u = (z, hb - 2)
                elif z >= 1:
                    prev_u = (z - 1, hb + 14)
                else:
                    prev_u = None
                for j in range(8):
                    h = 8 * hb + j
                    g_ = h // 16
                    epar = h % 2
                    sl8 = h % 8
                    w = []
                    if j == 0:
                        w.append((s_lk[bpar], 32 * (8 * z + hb // 2 + 1)))
                    if (z, h) > (0, 1):
                        w.append((s_ex6, EXh[(z, h - 2)] if h >= 2 else EXh[(z - 1, 126 + h)]))
                    if h <= 1:
                        w.append((s_mt, 8 * z + 8))
                    t("T", (lambda te, j=j, bpar=bpar, epar=epar:
                            te.matmul(pe6[epar][:, 0:128],
                                      stk[bpar][0:2, 128 * j:128 * j + 128],
                                      stk[bpar][0:2, 1024 + 128 * j:1024 + 128 * j + 128],
                                      start=True, stop=False)),
                      waits=w)
                    t("T", (lambda te, epar=epar:
                            te.matmul(pe6[epar][:, 0:128], self.id_bf[:, :],
                                      self.trineg[:, :], start=False, stop=True)),
                      inc=s_em)
                    EM[(z, h)] = s_em.n
                    w = [(s_em, EM[(z, h)])]
                    if h >= 8:
                        w.append((s_m2, M2d[(z, hb - 1)]))
                    elif z >= 1:
                        w.append((s_m2, M2d[(z - 1, 15)]))
                    t("S", (lambda s, epar=epar, sl8=sl8:
                            s.activation(lbf[:, 128 * sl8:128 * sl8 + 128],
                                         pe6[epar][:, 0:128], AF.Exp,
                                         bias=self.consts[:, 7:8], scale=1.0)),
                      waits=w, inc=s_ex6)
                    EXh[(z, h)] = s_ex6.n
                g_ = hb // 2
                t("G", (lambda g, g_=g_:
                        g.tensor_tensor(m2b[:, :], lbf[:, :],
                                        _ap(mtri, 128 * g_,
                                            [[1024, 128], [0, 8], [1, 128]]),
                                        ALU.mult)),
                  waits=[(s_ex6, EXh[(z, 8 * hb + 7)]), (s_mt, 8 * z + 8),
                         (s_ym, 16 * z + hb)], inc=s_m2)
                M2d[(z, hb)] = s_m2.n
                w = [(s_ym, 16 * z + hb)]
                if hb == 0:
                    w.append((s_lh[par], 16 * (z // 2 + 1)))
                t("V", (lambda v, z=z, hb=hb, par=par, bpar=bpar:
                        v.tensor_tensor(xtm6[:, 512 * bpar:512 * bpar + 512],
                                        hstm[:, INTER * par + 512 * hb:INTER * par + 512 * hb + 512],
                                        _ap(self.dt_tm, 128 * z + 8 * hb,
                                            [[TOK, 128], [1, 8], [0, 64]]),
                                        ALU.mult)),
                  waits=w, inc=s_xt)
                XT[(z, hb)] = s_xt.n
                for j in range(8):
                    h = 8 * hb + j
                    g_ = h // 16
                    sl8 = h % 8
                    w = []
                    if j == 0:
                        w = [(s_m2, M2d[(z, hb)]), (s_xt, XT[(z, hb)])]
                        if prev_u is not None:
                            w.append((s_yg, GE[prev_u]))
                        if z >= 1:
                            w.append((s_yv, EV[(z - 1, hb)]))
                    t("T", (lambda te, j=j, bpar=bpar, sl8=sl8:
                            te.matmul(pyd[bpar][:, 64 * j:64 * j + 64],
                                      m2b[:, 128 * sl8:128 * sl8 + 128],
                                      xtm6[:, 512 * bpar + 64 * j:512 * bpar + 64 * j + 64],
                                      start=True, stop=True)),
                      waits=w)
                    t("T", (lambda te, h=h, j=j, g_=g_, z=z, bpar=bpar:
                            te.matmul(pyo[bpar][:, 64 * j:64 * j + 64],
                                      self.bc_post[:, 1024 * (8 + g_) + 128 * z:1024 * (8 + g_) + 128 * z + 128],
                                      self.prev_bf[:, 64 * h:64 * h + 64],
                                      start=True, stop=True)),
                      inc=(s_ym if j == 7 else None))
                t("V", (lambda v, z=z, hb=hb, bpar=bpar:
                        v.tensor_tensor(t1y[:, 512 * bpar:512 * bpar + 512],
                                        pyo[bpar][:, :],
                                        _ap(self.expa_tm, 128 * z + 8 * hb,
                                            [[TOK, 128], [1, 8], [0, 64]]),
                                        ALU.mult)),
                  waits=[(s_ym, 16 * z + hb + 1)] +
                        ([(s_yg, GE[prev_u])] if prev_u is not None else []))
                t("V", (lambda v, bpar=bpar:
                        v.scalar_tensor_tensor(t1y[:, 512 * bpar:512 * bpar + 512],
                                               pyd[bpar][:, :], 1.0,
                                               t1y[:, 512 * bpar:512 * bpar + 512],
                                               ALU.mult, ALU.add)),
                  inc=s_t1)
                T1[(z, hb)] = s_t1.n
                t("V", (lambda v, z=z, hb=hb:
                        v.tensor_tensor(self.prev_bf[:, 512 * hb:512 * hb + 512],
                                        self.prev_bf[:, 512 * hb:512 * hb + 512],
                                        _ap(self.lam_bc, 128 * z + 8 * hb,
                                            [[TOK, 128], [1, 8], [0, 64]]),
                                        ALU.mult)),
                  waits=[(s_lz, 16 * (z + 1))])
                t("V", (lambda v, hb=hb:
                        v.tensor_add(self.prev_bf[:, 512 * hb:512 * hb + 512],
                                     self.prev_bf[:, 512 * hb:512 * hb + 512],
                                     sz6[:, 512 * hb:512 * hb + 512])),
                  inc=s_yv)
                EV[(z, hb)] = s_yv.n
                t("G", (lambda g, z=z, hb=hb, par=par:
                        g.tensor_tensor(ytm[:, 512 * hb:512 * hb + 512],
                                        hstm[:, INTER * par + 512 * hb:INTER * par + 512 * hb + 512],
                                        _ap(self.dbc, 8 * hb, [[128, 128], [1, 8], [0, 64]]),
                                        ALU.mult)),
                  waits=[(s_yv, EV[(z, hb)])] +
                        ([(s_gg, GG[z - 1])] if (z >= 1 and hb == 0) else []))
                t("G", (lambda g, hb=hb, bpar=bpar:
                        g.tensor_add(ytm[:, 512 * hb:512 * hb + 512],
                                     ytm[:, 512 * hb:512 * hb + 512],
                                     t1y[:, 512 * bpar:512 * bpar + 512])),
                  inc=s_yg)
                GE[(z, hb)] = s_yg.n
                if hb + 2 <= 15:
                    stack_load(z, hb + 2)
            t("V", (lambda v: v.tensor_tensor(stm[:, :], ytm[:, :], stm[:, :],
                                              ALU.mult)),
              waits=[(s_yg, GE[(z, 15)]), (s_ge, s_ge.n)], inc=s_gg)
            vG = s_gg.n
            GG[z] = vG
            for gr in range(8):
                t("S", (lambda s, gr=gr, par=par:
                        s.activation(hstm[:, INTER * par + 1024 * gr:INTER * par + 1024 * gr + 1024],
                                     stm[:, 1024 * gr:1024 * gr + 1024], AF.Square,
                                     accum_out=vacc[:, gr:gr + 1])),
                  waits=([(s_gg, vG)] if gr == 0 else []))
            t("S", (lambda s: s.copy(rstd[:, 0:1], vacc[:, 0:1])))
            t("S", (lambda s: s.activation(rstd[:, :], vacc[:, :], AF.Ln,
                                           bias=self.consts[:, 0:1],
                                           scale=1.0 / 1024.0)))
            t("S", (lambda s: s.activation(rstd[:, :], rstd[:, :], AF.Exp,
                                           bias=self.consts[:, 7:8], scale=-0.5)),
              inc=s_sq)
            SQ[z] = s_sq.n
            for gr in range(8):
                t("V", (lambda v, gr=gr:
                        v.tensor_scalar(gfm[:, 1024 * gr:1024 * gr + 1024],
                                        stm[:, 1024 * gr:1024 * gr + 1024],
                                        rstd[:, gr:gr + 1], None, ALU.mult)),
                  waits=([(s_sq, SQ[z])] if gr == 0 else []), inc=s_gg)
            vNM = s_gg.n
            for r_ in range(8):
                for j in range(8):
                    i = 8 * r_ + j
                    w = []
                    if j == 0:
                        w = [(s_gg, vNM)]
                        if r_ >= 2:
                            w.append((s_ne, s_ne.n - 1))
                        else:
                            w.append((s_ge, 8 * (z + 1)))
                    t("T", (lambda te, i=i, r_=r_, j=j:
                            te.transpose(pg[r_ % 2][:, 128 * j:128 * j + 128],
                                         gfm[:, 128 * i:128 * i + 128],
                                         self.id_bf[:, :])),
                      waits=w, inc=(s_gt if j == 7 else None))
                t("V", (lambda v, r_=r_:
                        v.tensor_tensor(nstg[:, 1024 * (r_ % 2):1024 * (r_ % 2) + 1024],
                                        pg[r_ % 2][:, :],
                                        _ap(nw_col, 8 * r_,
                                            [[64, 128], [1, 8], [0, 128]]),
                                        ALU.mult)),
                  waits=[(s_gt, s_gt.n), (s_nw, 16)] +
                        ([(s_ns[r_ % 2], 16 * (4 * z + r_ // 2))] if (8 * z + r_) >= 2 else []),
                  inc=s_ne)
                t("Y", (lambda s, z=z, r_=r_:
                        s.dma_start(_ap(self.normed_hbm, 1024 * r_ * TOK + 128 * z,
                                        [[TOK, 128], [128 * TOK, 8], [1, 128]]),
                                    _ap(nstg[:, :], 1024 * (r_ % 2),
                                        [[2048, 128], [128, 8], [1, 128]]))),
                  waits=[(s_ne, s_ne.n)], inc=s_ns[r_ % 2], inck=16, dma=True)
            NT[z] = s_gt.n
            NE[z] = s_ne.n
        self.drain([s_lh[0], s_lh[1], s_lz, s_lg, s_lk[0], s_lk[1],
                    s_ns[0], s_ns[1], s_nw])
        self.flush("p6")
        self.close_scope(sc)
        self.close_scope("prev")
        self.close_scope("bc")
        self.close_scope("dt")

    def outproj_block(self):
        sc = "p7"
        nfm = self.sb("nfm", [128, 64 * TOK], BF16, scope=sc)
        wq = self.sb("wq7", [128, 2 * 8192], BF16, scope=sc)
        sx = self.sb("sx7", [128, 2 * 512], BF16, scope=sc)
        x2b = self.sb("x2b", [128, 2 * 512], BF16, scope=sc)
        pw7 = [self.ps(f"pw7{i}", [128, 512], F32, scope=sc) for i in range(8)]
        s_nf = self.sem("snf7")
        s_wq = [self.sem("swq0"), self.sem("swq1")]
        s_qd = self.sem("sipu")
        s_od = self.sem("ssm")
        s_ev7 = self.sem("sce")
        s_sx = [self.sem("sst0"), self.sem("sst1")]
        s_x2 = [self.sem("sch0"), self.sem("sch1")]
        self.fresh([s_qd, s_od, s_ev7,
                    s_sx[0], s_sx[1], s_x2[0], s_x2[1]])

        self.begin()
        t = self.t
        t("G", lambda g: g.dma_start(
            nfm[:, :], _ap(self.normed_hbm, 0,
                           [[TOK, 128], [128 * TOK, 64], [1, TOK]])),
          inc=s_nf, inck=16, dma=True)
        for og in range(8):
            for qq in range(4):
                idx = 4 * og + qq
                par = idx % 2
                w = [(s_qd, idx - 1)] if idx >= 2 else []
                t("G", (lambda g, og=og, qq=qq, par=par:
                        g.dma_start(wq[:, 8192 * par:8192 * par + 8192],
                                    _ap(self.op_T, 2048 * qq * HID + 512 * og,
                                        [[HID, 128], [128 * HID, 16], [1, 512]]))),
                  waits=w, inc=s_wq[par], inck=16, dma=True)
            for qq in range(4):
                idx = 4 * og + qq
                par = idx % 2
                for i16 in range(16):
                    ic = 16 * qq + i16
                    w = []
                    if i16 == 0:
                        w = [(s_wq[par], 16 * (idx // 2 + 1))]
                        if qq == 0:
                            w.append((s_nf, 16))
                            if og >= 1:
                                w.append((s_ev7, 8 * og))
                    for tt in range(8):
                        t("T", (lambda te, og=og, ic=ic, tt=tt, par=par, i16=i16:
                                te.matmul(pw7[tt][:, :],
                                          nfm[:, 1024 * ic + 128 * tt:1024 * ic + 128 * tt + 128],
                                          wq[:, 8192 * par + 512 * i16:8192 * par + 512 * i16 + 512],
                                          start=(ic == 0), stop=(ic == 63))),
                          waits=(w if tt == 0 else []),
                          inc=(s_qd if (i16 == 15 and tt == 7) else None))
            OD = s_qd.n
            for tt in range(8):
                ep = tt % 2
                cnt = 8 * og + tt
                t("Y", (lambda s, og=og, tt=tt, ep=ep:
                        s.dma_start(sx[:, 512 * ep:512 * ep + 512],
                                    self.stream_hbm[128 * tt:128 * tt + 128,
                                                    512 * og:512 * og + 512])),
                  waits=([(s_ev7, cnt - 1)] if cnt >= 2 else []),
                  inc=s_sx[ep], inck=16, dma=True)
                t("V", (lambda v, tt=tt, ep=ep:
                        v.tensor_tensor(x2b[:, 512 * ep:512 * ep + 512],
                                        pw7[tt][:, :], sx[:, 512 * ep:512 * ep + 512],
                                        ALU.add)),
                  waits=[(s_qd, OD), (s_sx[ep], 16 * (s_sx[ep].n // 16))] +
                        ([(s_x2[ep], 16 * ((cnt - 2) // 2 + 1))] if cnt >= 2 else []),
                  inc=s_ev7)
                t("Y", (lambda s, og=og, tt=tt, ep=ep:
                        s.dma_start(self.stream_hbm[128 * tt:128 * tt + 128,
                                                    512 * og:512 * og + 512],
                                    x2b[:, 512 * ep:512 * ep + 512])),
                  waits=[(s_ev7, s_ev7.n)], inc=s_x2[ep], inck=16, dma=True)
        self.drain([s_nf, s_wq[0], s_wq[1], s_sx[0], s_sx[1], s_x2[0], s_x2[1]])
        self.flush("p7")
        self.close_scope(sc)

    def ln3_block(self):
        self.stream2 = self.sb("stream2", [128, NT * HID], BF16, scope="p89")
        self.xfm = self.sb("xfm2", [128, NCH * TOKH], BF16, scope="p89x", side="right")
        s_s2 = self.sem("sspill")
        self.fresh([s_s2])
        self.begin()
        for tt in range(NT):
            self.t("Y", (lambda s, tt=tt:
                         s.dma_start(self.stream2[:, HID * tt:HID * tt + HID],
                                     self.stream_hbm[128 * tt:128 * tt + 128, :])),
                   inc=s_s2, inck=16, dma=True)
        self.t("Y", lambda s: None, waits=[(s_s2, s_s2.n)], dma=True)
        self.t("G", lambda g: None, waits=[(s_s2, s_s2.n)], dma=True)
        self.flush("p8ld")
        self.ln_block("c", src_tiles=lambda tt: self.stream2[:, tt * HID:(tt + 1) * HID],
                      ntile=8)

    def final_block(self):
        s_fin = self.sem("sfin")
        self.begin()
        for s in self._all_sems[:-1]:
            if s.n > 0:
                self.t("Y", lambda y: None, waits=[(s, s.n)])
        if self.debug and self.n_phase >= 1 and "xfm" in self._scopes:
            self.t("Y", lambda s: s.dma_start(self.dbg_xfm[:, :], self.xfm[:, :]),
                   inc=s_fin, inck=16)
        if self.debug and "prev" in self._scopes:
            self.t("G", lambda g: g.dma_start(self.dbg_a[:, 0:2048],
                                              self.prev_bf[:, 0:2048]),
                   inc=s_fin, inck=16, dma=True)
        if self.debug and "stream" in self._scopes:
            self.t("Y", lambda s: s.dma_start(self.dbg_stream[:, 0:NT * HID],
                                              self.stream[:, :]), inc=s_fin, inck=16)
            self.t("Y", lambda s: s.dma_start(self.dbg_stream[0:3, NT * HID:],
                                              self.halo[0:3, :]), inc=s_fin, inck=16)
        if s_fin.n:
            self.t("Y", lambda s: None, waits=[(s_fin, s_fin.n)])
        self.flush("fin")


def host_prep(inputs):
    f32 = np.float32
    x = np.asarray(inputs["L0_attn_x"], f32)
    ipw = np.asarray(inputs["L0_attn_in_proj_w"], f32)
    convw = np.asarray(inputs["L0_attn_conv_w"], f32)
    convb = np.asarray(inputs["L0_attn_conv_b"], f32)
    dtb = np.asarray(inputs["L0_attn_dt_bias"], f32)
    alog = np.asarray(inputs["L0_attn_A_log"], f32)
    dd = np.asarray(inputs["L0_attn_D"], f32)
    normw = np.asarray(inputs["L0_attn_norm_w"], f32)
    opw = np.asarray(inputs["L0_attn_out_proj_w"], f32)
    wb = np.asarray(inputs["L0_ffn_W_base"], f32)
    bb = np.asarray(inputs["L0_ffn_b_base"], f32)
    lA = np.asarray(inputs["L0_ffn_lora_A"], f32)
    lB = np.asarray(inputs["L0_ffn_lora_B"], f32)
    rW = np.asarray(inputs["L0_ffn_router_W"], f32)
    rb = np.asarray(inputs["L0_ffn_router_b"], f32)
    prior = np.asarray(inputs["L0_ffn_domain_prior"], f32)
    act = np.asarray(inputs["L0_ffn_active"]).astype(np.int64)

    shared = {
        "ip_T": np.ascontiguousarray(ipw.T),
        "wb_T": np.ascontiguousarray(wb.T),
        "op_T": np.ascontiguousarray(opw.T),
        "w36": np.ascontiguousarray(
            np.concatenate([lA[act].reshape(32, HID), rW[act]], axis=0)),
        "bcatT": np.ascontiguousarray(
            lB[act].transpose(0, 2, 1).reshape(32, HID)),
        "b_rep": np.ascontiguousarray(np.tile(bb[None, :], (128, 1))),
        "normw_col": np.ascontiguousarray(normw.reshape(64, 128).T),
        "convw_cols": np.ascontiguousarray(
            convw.reshape(80, 128, 4).transpose(1, 0, 2).reshape(128, 320)),
        "convb_cols": np.ascontiguousarray(
            convb.reshape(80, 128).T),
        "dtb_col": np.ascontiguousarray(dtb[:, None]),
        "alog_col": np.ascontiguousarray(alog[:, None]),
        "d_bc_in": np.ascontiguousarray(np.tile(dd[None, :], (128, 1))),
        "prior_row": np.ascontiguousarray(prior[act][None, :]),
        "rb_row": np.ascontiguousarray(rb[act][None, :]),
        "ident_bf": np.eye(128, dtype=BF),
        "ident_f32": np.eye(128, dtype=f32),
        "tri01_rep": np.ascontiguousarray(np.triu(np.ones((128, 128), f32))),
        "trineg_bf": np.where(np.triu(np.ones((128, 128), bool)),
                              np.zeros((128, 128)), -1e9).astype(BF),
    }
    per_core = []
    for c in range(8):
        b, q = c // 4, c % 4
        xs = np.zeros((TOKH, HID), f32)
        t0 = 1024 * q
        xs[3:, :] = x[b, t0:t0 + 1024, :]
        if q > 0:
            xs[0:3, :] = x[b, t0 - 3:t0, :]
        hm = np.zeros((4, 1), f32)
        hm[:, 0] = 0.0 if q == 0 else 1.0
        selT = np.zeros((8, 8), f32)
        rmask = np.zeros((8, 1), f32)
        for k in range(8):
            if k // 4 == b and (k % 4) < q:
                rmask[k, 0] = 1.0
                for m in range(8):
                    if m // 4 == b and (k % 4) < (m % 4) < q:
                        selT[m, k] = 1.0
        d = dict(shared)
        d.update({"x_in": xs, "halo_mask": hm, "selT_in": selT, "rmask_in": rmask})
        per_core.append(d)
    return per_core


_CACHED = {}


def _get_nc(n_phase=99, debug=False):
    key = (n_phase, debug)
    if key not in _CACHED:
        _CACHED[key] = Builder(n_phase, debug).build()
    return _CACHED[key]


def kernel(**inputs):
    per_core = host_prep(inputs)
    nc = _get_nc()
    res = run_bass_kernel_spmd(nc, per_core, core_ids=list(range(8)))
    outs = [res.results[c]["out"] for c in range(8)]
    full = np.zeros((2, 4096, HID), np.float32)
    for c in range(8):
        b, q = c // 4, c % 4
        full[b, 1024 * q:1024 * q + 1024, :] = outs[c]
    return full
